# version 12
# baseline (speedup 1.0000x reference)
import contextlib
import numpy as np
import ml_dtypes

import concourse.bass as bass
import concourse.mybir as mybir
from concourse.bass_utils import run_bass_kernel_spmd

F32 = mybir.dt.float32
BF16 = mybir.dt.bfloat16
F16 = mybir.dt.float16
U16 = mybir.dt.uint16
I32 = mybir.dt.int32
AF = mybir.ActivationFunctionType
ALU = mybir.AluOpType
AX = mybir.AxisListType
NPBF16 = ml_dtypes.bfloat16


class Cfg:
    def __init__(self, T=4096, I=5632):
        self.C = 2048
        self.T = T
        self.H = 16
        self.DH = 128
        self.DL = 64
        self.I = I
        self.NCC = self.C // 128
        self.SLAB = T // 8 if T >= 1024 else 128
        self.NSLAB = T // self.SLAB
        self.QT = 512
        self.NQT = T // 512
        self.TT = T // 8
        self.NTT = 8
        self.NOC = I // 128
        self.NBC_GU = self.C // 256
        self.NBC_DN = I // 256
        self.EXP_SHIFT = 12.0
        self.EPS_LN = 1e-5
        self.EPS_RMS = 1e-6
        assert T % 512 == 0 and self.SLAB % 128 == 0 and I % 256 == 0


def _perm_half(n):
    p = np.empty(n, np.int64)
    for f in range(n):
        p[(f & 1) * (n // 2) + (f >> 1)] = f
    return p


def host_prep(inputs, cfg):
    C, H, DH, DL, I = cfg.C, cfg.H, cfg.DH, cfg.DL, cfg.I
    fo_gu = _perm_half(C)
    fo_dn = _perm_half(I)
    do_h = _perm_half(DH)
    lo_l = _perm_half(DL)

    d = {}
    wq = np.asarray(inputs["L0_attn_wq"])
    col_perm = np.concatenate([h * DH + do_h for h in range(H)])
    d["wq"] = np.ascontiguousarray(wq[np.ix_(fo_gu, col_perm)], np.float32)
    d["wk"] = np.ascontiguousarray(
        np.asarray(inputs["L0_attn_wk_lat"])[np.ix_(fo_gu, lo_l)], np.float32)
    d["wv"] = np.ascontiguousarray(np.asarray(inputs["L0_attn_wv_lat"])[fo_gu], np.float32)
    ke = np.asarray(inputs["L0_attn_k_expand"])[:, lo_l][:, :, do_h]
    d["ke"] = np.ascontiguousarray(ke.reshape(H * DL, DH), np.float32)
    d["ve"] = np.ascontiguousarray(
        np.asarray(inputs["L0_attn_v_expand"]).reshape(H * DL, DH), np.float32)
    d["pw"] = np.ascontiguousarray(np.asarray(inputs["L0_attn_proj_w"]), np.float32)

    gup = np.asarray(inputs["L0_ffn_gate_up_packed"])
    gus = np.asarray(inputs["L0_ffn_gate_up_scales"])
    dnp_ = np.asarray(inputs["L0_ffn_down_packed"])
    dns = np.asarray(inputs["L0_ffn_down_scales"])

    def widen(b):
        w = np.zeros(b.shape + (2,), np.uint8)
        w[..., 0] = b
        return np.ascontiguousarray(w.view(np.uint16)[..., 0])

    gu_col = np.concatenate([fo_dn, I + fo_dn])
    gu_bytes = gup.view(np.uint8).reshape(2 * I, C // 2)
    d["gub"] = widen(np.ascontiguousarray(gu_bytes[gu_col].T))
    d["gus"] = np.ascontiguousarray(gus[:, gu_col])
    dn_bytes = dnp_.view(np.uint8).reshape(C, I // 2)
    d["dnb"] = widen(np.ascontiguousarray(dn_bytes.T))
    d["dns"] = np.ascontiguousarray(dns)
    return d


def rope_tables(t, dim):
    inv_freq = 1.0 / 10000.0 ** (np.arange(0, dim, 2, dtype=np.float32) / dim)
    ang = np.arange(t, dtype=np.float32)[:, None] * inv_freq[None, :]
    return (np.ascontiguousarray(np.cos(ang).T).astype(NPBF16),
            np.ascontiguousarray(np.sin(ang).T).astype(NPBF16))


def rot_mat_T(dim):
    half = dim // 2
    m = np.zeros((dim, dim), NPBF16)
    for i in range(half):
        m[i, half + i] = 1.0
        m[half + i, i] = -1.0
    return m


ENGS = ("pe", "act", "dve", "pool", "sp")


class Buf:
    __slots__ = ("name", "w", "r", "war", "dsem", "dcnt", "pool_dma")

    def __init__(self, name):
        self.name = name
        self.w = []
        self.r = []
        self.war = []
        self.dsem = None
        self.dcnt = 0
        self.pool_dma = False

    def dep_write(self, fresh):
        if fresh:
            self.war = self.w + self.r
            self.w = []
            self.r = []
        return list(self.war)

    def dep_read(self):
        return list(self.w)


class Prog:
    def __init__(self, nc, stack):
        self.nc = nc
        self.stack = stack
        self.sem = {e: stack.enter_context(nc.semaphore(f"S_{e}")) for e in ENGS}
        self.cnt = {e: 0 for e in ENGS}
        self.recs = {e: [] for e in ENGS}
        self.all_dma_bufs = []

    def op(self, eng, fn, rd=(), wr=(), fresh=True, waits=()):
        wts = list(waits)
        for b in rd:
            wts += b.dep_read()
        for b in wr:
            wts += b.dep_write(fresh)
        self.cnt[eng] += 1
        ev = (self.sem[eng], self.cnt[eng])
        self.recs[eng].append((wts, fn, ev, 1))
        for b in rd:
            b.r.append(ev)
        for b in wr:
            b.w.append(ev)
        return ev

    def dma(self, queue, out_ap, in_ap, dst, src=(), fresh=True, waits=(), dram_dst=None,
            accum=False):
        if dst.dsem is None:
            dst.dsem = self.stack.enter_context(self.nc.semaphore(f"D_{dst.name}"))
            self.all_dma_bufs.append(dst)
            if queue == "pool":
                dst.pool_dma = True
        wts = list(waits)
        for b in src:
            wts += b.dep_read()
        wts += dst.dep_write(fresh)
        if dram_dst is not None:
            wts += dram_dst.dep_write(False)
        dst.dcnt += 16
        ev = (dst.dsem, dst.dcnt)

        def fn(e, _o=out_ap, _i=in_ap, _a=accum):
            if _a:
                return e.dma_start(out=_o, in_=_i, accum_op=mybir.AluOpType.add)
            return e.dma_start(out=_o, in_=_i)

        self.recs[queue].append((wts, fn, ev, 16))
        for b in src:
            b.r.append(ev)
        dst.w.append(ev)
        if dram_dst is not None:
            dram_dst.w.append(ev)
        return ev

    def store(self, queue, out_ap, in_ap, src_buf, dram_dst):
        key = id(src_buf)
        if not hasattr(self, "_stslots"):
            self._stslots = {}
        slot = self._stslots.get(key)
        if slot is None:
            slot = Buf("stsl_" + src_buf.name)
            self._stslots[key] = slot
        return self.dma(queue, out_ap, in_ap, slot, src=[src_buf], fresh=False,
                        dram_dst=dram_dst)

    def barrier(self, include_pool=True):
        engs = [e for e in ENGS if include_pool or e != "pool"]
        wts = [(self.sem[e], self.cnt[e]) for e in engs if self.cnt[e] > 0]
        wts += [(b.dsem, b.dcnt) for b in self.all_dma_bufs
                if include_pool or not b.pool_dma]
        self.cnt["sp"] += 1
        ev = (self.sem["sp"], self.cnt["sp"])
        self.recs["sp"].append((wts, lambda e: e.nop(), ev, 1))
        for e in engs:
            if e == "sp":
                continue
            self.cnt[e] += 1
            self.recs[e].append(([ev], lambda en: en.nop(), (self.sem[e], self.cnt[e]), 1))

    def final_wait(self, buf):
        self.recs["sp"].append((buf.dep_read(), None, None, 0))

    def emit(self, block):
        hmap = {"pe": block.tensor, "act": block.scalar, "dve": block.vector,
                "pool": block.gpsimd, "sp": block.sync}

        def mk(eng):
            recs = self.recs[eng]

            def f(e):
                seen = {}
                for wts, fn, ev, inc in recs:
                    byname = {}
                    for (sem, val) in wts:
                        k = id(sem)
                        if byname.get(k, (None, -1))[1] < val:
                            byname[k] = (sem, val)
                    for k, (sem, val) in byname.items():
                        if seen.get(k, -1) >= val:
                            continue
                        e.wait_ge(sem, val)
                        seen[k] = val
                    if fn is None:
                        continue
                    inst = fn(e)
                    if inc:
                        inst.then_inc(ev[0], inc)
            return f

        for eng in ENGS:
            hmap[eng](mk(eng))


def build_program(cfg, dbg=False, upto=9, f2n=9):
    nc = bass.Bass("TRN2", target_bir_lowering=False)
    C, T, H, DH, DL, I = cfg.C, cfg.T, cfg.H, cfg.DH, cfg.DL, cfg.I
    NCC, SLAB, NSLAB = cfg.NCC, cfg.SLAB, cfg.NSLAB
    QT, NQT, TT, NTT, NOC = cfg.QT, cfg.NQT, cfg.TT, cfg.NTT, cfg.NOC
    SCALE = float(1.0 / np.sqrt(DH))

    xin = nc.dram_tensor("x", [T, C], F32, kind="ExternalInput").ap()
    wq_d = nc.dram_tensor("wq", [C, C], F32, kind="ExternalInput").ap()
    wk_d = nc.dram_tensor("wk", [C, DL], F32, kind="ExternalInput").ap()
    wv_d = nc.dram_tensor("wv", [C, DL], F32, kind="ExternalInput").ap()
    ke_d = nc.dram_tensor("ke", [H * DL, DH], F32, kind="ExternalInput").ap()
    ve_d = nc.dram_tensor("ve", [H * DL, DH], F32, kind="ExternalInput").ap()
    pw_d = nc.dram_tensor("pw", [C, C], F32, kind="ExternalInput").ap()
    gub_d = nc.dram_tensor("gub", [C // 2, 2 * I], U16, kind="ExternalInput").ap()
    gus_d = nc.dram_tensor("gus", [C // 128, 2 * I], F16, kind="ExternalInput").ap()
    dnb_d = nc.dram_tensor("dnb", [I // 2, C], U16, kind="ExternalInput").ap()
    dns_d = nc.dram_tensor("dns", [I // 128, C], F16, kind="ExternalInput").ap()
    out_d = nc.dram_tensor("out", [T, C], F32, kind="ExternalOutput").ap()

    sk = "ExternalOutput" if dbg else "Internal"
    xn_d = nc.dram_tensor("xn_s", [C, T], BF16, kind=sk).ap()
    q_d = nc.dram_tensor("q_s", [C, T], BF16, kind=sk).ap()
    y_d = nc.dram_tensor("y_s", [C, T], BF16, kind=sk).ap()
    attn_d = nc.dram_tensor("attn_s", [T, C], F32, kind=sk).ap()
    wgu_sd = nc.dram_tensor("wgu_s", [C, 2 * I], BF16, kind=sk).ap()
    wdn_sd = nc.dram_tensor("wdn_s", [I, C], BF16, kind=sk).ap()
    h_sd = nc.dram_tensor("h_s", [I, T], BF16, kind=sk).ap()
    wqb_sd = nc.dram_tensor("wqb_s", [128, (C // 128) * C], BF16, kind="Internal").ap()

    cqn, sqn = rope_tables(T, DH)
    ckn, skn = rope_tables(T, DL)
    consts_np = {
        "ident": np.eye(128, dtype=NPBF16),
        "tri": np.triu(np.ones((128, 128), NPBF16)),
        "onesb": np.ones((128, 1), NPBF16),
        "onesf": np.ones((1, 128), np.float32),
        "onef1": np.ones((1, 1), np.float32),
        "cqb": np.vstack([cqn, cqn]), "sqb": np.vstack([sqn, sqn]),
        "ckb": np.vstack([ckn, ckn]), "skb": np.vstack([skn, skn]),
        "rotq": rot_mat_T(DH), "rotk": rot_mat_T(DL),
    }
    consts_d = {k: nc.inline_tensor(v, name=k + "_c") for k, v in consts_np.items()}

    stack = contextlib.ExitStack()
    with stack:
        P = Prog(nc, stack)

        def mksb(st_, name, shape, dt):
            return st_.enter_context(nc.sbuf_tensor(name, shape, dt))

        def mkps(st_, name, shape, dt=F32):
            return st_.enter_context(nc.psum_tensor(name, shape, dt))

        cs = {}
        cB = {}
        ACM_CONSTS = ("cqb", "sqb", "ckb", "skb", "tri", "rotq", "rotk")
        for k, v in consts_np.items():
            if k in ACM_CONSTS:
                continue
            cs[k] = mksb(stack, k, list(v.shape), BF16 if v.dtype == NPBF16 else F32)
            cB[k] = Buf(k)
            P.dma("sp", cs[k][:], consts_d[k].ap(), cB[k])
        for k, val in (("nbias", -cfg.EXP_SHIFT), ("epsb", cfg.EPS_LN), ("epsr", cfg.EPS_RMS)):
            cs[k] = mksb(stack, k, [128, 1], F32)
            cB[k] = Buf(k)
            P.op("dve", lambda e, k=k, val=val: e.memset(cs[k][:], val), wr=[cB[k]])

        dB = {k: Buf(k) for k in ("xn", "q", "y", "attn", "wgu", "wdn", "h", "wqb")}
        outB = Buf("out")

        acm = contextlib.ExitStack()
        stack.enter_context(acm)
        for k in ("tri",):
            v = consts_np[k]
            cs[k] = mksb(acm, k, list(v.shape), BF16 if v.dtype == NPBF16 else F32)
            cB[k] = Buf(k)
            P.dma("sp", cs[k][:], consts_d[k].ap(), cB[k])
        klat_s = mksb(acm, "klat", [DL, T], BF16)
        vlat_s = mksb(acm, "vlat", [DL, T], BF16)
        klatB, vlatB = Buf("klat"), Buf("vlat")

        OSL = 1024
        f0s = contextlib.ExitStack()
        stack.enter_context(f0s)
        f0t = {}
        for nm, dt in (("d1", U16), ("d2a", U16), ("d2b", U16), ("d3a", BF16),
                       ("d3b", BF16), ("d4", U16), ("d5a", BF16), ("d5b", BF16),
                       ("p1a", BF16), ("p1b", BF16), ("p2", BF16)):
            f0t[nm] = (mksb(f0s, "f0" + nm, [128, OSL], dt), Buf("f0" + nm))
        f0byt = [(mksb(f0s, f"f0byt{i}", [128, OSL], U16), Buf(f"f0byt{i}")) for i in range(2)]
        f0sm = [(mksb(f0s, f"f0sm{i}", [128, OSL], F16), Buf(f"f0sm{i}")) for i in range(2)]
        f0wo = [(mksb(f0s, f"f0wo{i}", [128, OSL], BF16), Buf(f"f0wo{i}")) for i in range(2)]

        f0_units = []

        def mk_f0_unit(bytes_d, scales_d, w_out_d, wB, ncols, bc, sl, hi, ui):
            c0 = sl * OSL
            n = min(ncols, c0 + OSL) - c0
            byt, bB = f0byt[(ui // 2) % 2]
            smt, sB = f0sm[(ui // 2) % 2]
            sh = 2 if hi else 6
            msk = 0x80 if hi else 0x8
            sgs = -0.015625 if hi else -0.25
            ab = "b" if (ui % 2) else "a"
            d1, b1 = f0t["d1"]
            d2, b2 = f0t["d2" + ab]
            d3, b3 = f0t["d3" + ab]
            d4, b4 = f0t["d4"]
            d5, b5 = f0t["d5" + ab]
            p1, pb1 = f0t["p1" + ab]
            p2, pb2 = f0t["p2"]
            wo, woB = f0wo[ui % 2]

            def load():
                if not hi:
                    P.dma("pool", byt[:, 0:n], bytes_d[bc * 128:(bc + 1) * 128, c0:c0 + n], bB)
                    P.dma("pool", smt[0:64, 0:n],
                          scales_d[2 * bc:2 * bc + 1, c0:c0 + n].broadcast_to((64, n)), sB)
                    P.dma("pool", smt[64:128, 0:n],
                          scales_d[2 * bc + 1:2 * bc + 2, c0:c0 + n].broadcast_to((64, n)),
                          sB, fresh=False)

            def work():
                orow = bc * 128 + hi * ((C if w_out_d is wgu_sd else I) // 2)
                nh = n // 2
                mask32 = 0x01C001C0
                msk32 = (msk << 16) | msk
                sgsh = 12 if msk == 0x8 else 8
                P.op("dve", lambda e, nh=nh: e.tensor_scalar(
                    d1[:, 0:n].bitcast(I32), byt[:, 0:n].bitcast(I32), sh, mask32,
                    ALU.logical_shift_left, ALU.bitwise_and), rd=[bB], wr=[b1])
                P.op("dve", lambda e: e.tensor_scalar(
                    d2[:, 0:n], d1[:, 0:n], 0x3F00, None, ALU.add), rd=[b1], wr=[b2])
                P.op("dve", lambda e: e.tensor_scalar(
                    d3[:, 0:n], d2[:, 0:n].bitcast(BF16), 2.0, 1.0,
                    ALU.mult, ALU.subtract), rd=[b2], wr=[b3])
                P.op("dve", lambda e, msk32=msk32: e.tensor_scalar(
                    d4[:, 0:n].bitcast(I32), byt[:, 0:n].bitcast(I32), msk32, None,
                    ALU.bitwise_and), rd=[bB], wr=[b4])
                P.op("dve", lambda e, sgsh=sgsh: e.tensor_scalar(
                    d5[:, 0:n].bitcast(I32), d4[:, 0:n].bitcast(I32), sgsh, 0x3F803F80,
                    ALU.logical_shift_left, ALU.bitwise_or), rd=[b4], wr=[b5])
                P.op("dve", lambda e: e.tensor_tensor(
                    p1[:, 0:n], d2[:, 0:n].bitcast(BF16), d3[:, 0:n], ALU.min),
                    rd=[b2, b3], wr=[pb1])
                P.op("pool", lambda e: e.tensor_tensor(
                    p2[:, 0:n], p1[:, 0:n], d5[:, 0:n], ALU.mult),
                    rd=[pb1, b5], wr=[pb2])
                P.op("pool", lambda e: e.tensor_tensor(
                    wo[:, 0:n], p2[:, 0:n], smt[:, 0:n], ALU.mult),
                    rd=[pb2, sB], wr=[woB])
                P.store("pool", w_out_d[orow:orow + 128, c0:c0 + n], wo[:, 0:n], woB, wB)
            return load, work

        ui = 0
        for (bytes_d, scales_d, w_out_d, wB, ncols, nbc) in (
                (gub_d, gus_d, wgu_sd, dB["wgu"], 2 * I, cfg.NBC_GU),
                (dnb_d, dns_d, wdn_sd, dB["wdn"], C, cfg.NBC_DN)):
            nsl = (ncols + OSL - 1) // OSL
            for bc in range(nbc):
                for sl in range(nsl):
                    for hi in (0, 1):
                        f0_units.append(mk_f0_unit(bytes_d, scales_d, w_out_d, wB,
                                                   ncols, bc, sl, hi, ui))
                        ui += 1

        f0_state = {"i": 0}

        def f0_pump(k):
            for _ in range(k):
                i = f0_state["i"]
                if i >= len(f0_units):
                    return
                if i == 0:
                    f0_units[0][0]()
                    f0_units[1][0]()
                if i + 2 < len(f0_units):
                    f0_units[i + 2][0]()
                f0_units[i][1]()
                f0_state["i"] = i + 1


        a1 = contextlib.ExitStack()
        with a1:
            for k in ("cqb", "sqb", "ckb", "skb", "rotq", "rotk"):
                v = consts_np[k]
                cs[k] = mksb(a1, k, list(v.shape), BF16 if v.dtype == NPBF16 else F32)
                cB[k] = Buf(k)
                P.dma("sp", cs[k][:], consts_d[k].ap(), cB[k])
            XT = SLAB // 128
            NQM = max(1, SLAB // 512)
            QW = min(512, SLAB)
            xt_s = [mksb(a1, f"xts{i}", [128, C], F32) for i in range(2)]
            sq_s2 = [mksb(a1, f"sqs{i}", [128, C], BF16) for i in range(2)]
            xnm_s = [mksb(a1, f"xnm{i}", [128, C], BF16) for i in range(2)]
            st_s2 = [mksb(a1, f"sts{i}", [128, 10], F32) for i in range(2)]
            xnT_s = [mksb(a1, f"xnT{i}", [128, NCC * SLAB], BF16) for i in range(2)]
            wqf_s = [mksb(a1, f"wqf{i}", [128, NCC * 128], F32) for i in range(2)]
            wqb_s = [mksb(a1, f"wqb{i}", [128, NCC * 128], BF16) for i in range(2)]
            qsc_s2 = [mksb(a1, f"qsc{i}", [128, SLAB], BF16) for i in range(2)]
            qro_s = [mksb(a1, f"qro{i}", [128, SLAB], BF16) for i in range(2)]
            rs_s = [mksb(a1, f"rs{i}", [128, SLAB], BF16) for i in range(2)]
            wkvf_s = mksb(a1, "wkvf", [128, NCC * 2 * DL], F32)
            wkvb_s = mksb(a1, "wkvb", [128, NCC * 2 * DL], BF16)
            kvsc_s = mksb(a1, "kvsc", [DL, SLAB], BF16)

            trps = [mkps(a1, f"trp{i}", [128, 128], BF16) for i in range(4)]
            qps = [mkps(a1, f"qp{i}", [128, SLAB]) for i in range(2)]
            kvps = mkps(a1, "kvp", [DL, SLAB])
            rops = mkps(a1, "rop", [128, SLAB])

            xtB = [Buf(f"xts{i}") for i in range(2)]
            sqB2 = [Buf(f"sqs{i}") for i in range(2)]
            xnmB = [Buf(f"xnm{i}") for i in range(2)]
            stB2 = [[Buf(f"st{j}_{i}") for i in range(10)] for j in range(2)]
            xnTB = [Buf(f"xnT{i}") for i in range(2)]
            wqfB = [Buf(f"wqf{i}") for i in range(2)]
            wqbB = [Buf(f"wqb{i}") for i in range(2)]
            qscB2 = [Buf(f"qsc{i}") for i in range(2)]
            qroB = [Buf(f"qro{i}") for i in range(2)]
            rsB = [Buf(f"rs{i}") for i in range(2)]
            wkvfB = Buf("wkvf")
            wkvbB = Buf("wkvb")
            kvscB = Buf("kvsc")
            trpB = [Buf(f"trp{i}") for i in range(4)]
            qpsB = [Buf(f"qp{i}") for i in range(2)]
            kvpsB = Buf("kvp")
            ropsB = Buf("rop")

            P.dma("sp", wkvf_s[:, 0:NCC * DL].rearrange("p (c d) -> p c d", c=NCC),
                  wk_d.rearrange("(c p) d -> p c d", p=128), wkvfB)
            P.dma("sp", wkvf_s[:, NCC * DL:2 * NCC * DL].rearrange("p (c d) -> p c d", c=NCC),
                  wv_d.rearrange("(c p) d -> p c d", p=128), wkvfB, fresh=False)
            P.op("dve", lambda e: e.tensor_copy(wkvb_s[:], wkvf_s[:]),
                 rd=[wkvfB], wr=[wkvbB])

            for s in range(NSLAB if upto >= 1 else 0):
                p = s % 2
                t0 = s * SLAB
                for xt in range(XT):
                    xp = xt % 2
                    rows = t0 + xt * 128
                    P.dma("sp", xt_s[xp][:], xin[rows:rows + 128, :], xtB[xp])
                    st_s = st_s2[xp]
                    stB = stB2[xp]
                    sq_l = sq_s2[xp]
                    sqB = sqB2[xp]
                    P.op("dve", lambda e, xp=xp, st_s=st_s: e.tensor_reduce(
                        st_s[:, 0:1], xt_s[xp][:], AX.X, ALU.add, negate=True),
                        rd=[xtB[xp]], wr=[stB[0]])
                    P.op("act", lambda e, xp=xp, st_s=st_s, sq_l=sq_l: e.activation(
                        sq_l[:], xt_s[xp][:], AF.Square, accum_out=st_s[:, 1:2]),
                        rd=[xtB[xp]], wr=[sqB, stB[1]])
                    P.op("dve", lambda e, st_s=st_s: e.tensor_scalar(
                        st_s[:, 2:3], st_s[:, 0:1], 1.0 / C, None, ALU.mult),
                        rd=[stB[0]], wr=[stB[2]])
                    P.op("dve", lambda e, st_s=st_s: e.tensor_scalar(
                        st_s[:, 3:4], st_s[:, 1:2], 1.0 / C, None, ALU.mult),
                        rd=[stB[1]], wr=[stB[3]])
                    P.op("dve", lambda e, st_s=st_s: e.tensor_scalar(
                        st_s[:, 4:5], st_s[:, 2:3], st_s[:, 2:3], None, ALU.mult),
                        rd=[stB[2]], wr=[stB[4]])
                    P.op("dve", lambda e, st_s=st_s: e.tensor_tensor(
                        st_s[:, 5:6], st_s[:, 3:4], st_s[:, 4:5], ALU.subtract),
                        rd=[stB[3], stB[4]], wr=[stB[5]])
                    P.op("act", lambda e, st_s=st_s: e.activation(
                        st_s[:, 6:7], st_s[:, 5:6], AF.Sqrt, bias=cs["epsb"][:, 0:1]),
                        rd=[stB[5], cB["epsb"]], wr=[stB[6]])
                    P.op("dve", lambda e, st_s=st_s: e.reciprocal(st_s[:, 7:8], st_s[:, 6:7]),
                         rd=[stB[6]], wr=[stB[7]])
                    P.op("dve", lambda e, xp=xp, st_s=st_s: e.tensor_scalar(
                        xnm_s[xp][:], xt_s[xp][:], st_s[:, 2:3], st_s[:, 7:8],
                        ALU.add, ALU.mult), rd=[xtB[xp], stB[2], stB[7]], wr=[xnmB[xp]])
                    f0_pump(2)
                    for cc in range(NCC):
                        tp = cc % 4
                        par = cc // (NCC // 2)
                        base = (cc % (NCC // 2)) * 256 + par
                        P.op("pe", lambda e, tp=tp, xp=xp, base=base: e.transpose(
                            trps[tp][:], xnm_s[xp][:, base:base + 255:2], cs["ident"][:]),
                            rd=[xnmB[xp], cB["ident"]], wr=[trpB[tp]])
                        if cc % 4 >= 2:
                            P.op("dve", lambda e, tp=tp, p=p, cc=cc, xt=xt: e.tensor_copy(
                                xnT_s[p][:, cc * SLAB + xt * 128: cc * SLAB + xt * 128 + 128],
                                trps[tp][:]), rd=[trpB[tp]], wr=[xnTB[p]],
                                fresh=(cc == 0 and xt == 0))
                        else:
                            P.op("act", lambda e, tp=tp, p=p, cc=cc, xt=xt: e.copy(
                                xnT_s[p][:, cc * SLAB + xt * 128: cc * SLAB + xt * 128 + 128],
                                trps[tp][:]), rd=[trpB[tp]], wr=[xnTB[p]],
                                fresh=(cc == 0 and xt == 0))
                for g4 in range(4):
                    P.store("act",
                            xn_d.rearrange("(c p) t -> p c t", p=128)[:, g4::4, t0:t0 + SLAB],
                            xnT_s[p][:].rearrange("p (c t) -> p c t", c=NCC)[:, g4::4, :],
                            xnTB[p], dB["xn"])
                def rope_tail(dc, t0):
                    qp_ = dc % 2
                    qsc_l, qscB_l = qsc_s2[qp_], qscB2[qp_]
                    P.op("pe", lambda e, qsc_l=qsc_l: e.matmul(
                        rops[:], cs["rotq"][:], qsc_l[:], start=True, stop=True),
                        rd=[qscB_l, cB["rotq"]], wr=[ropsB])
                    P.op("dve", lambda e, t0=t0, qsc_l=qsc_l: e.tensor_tensor(
                        rs_s[0][:], qsc_l[:], cs["cqb"][:, t0:t0 + SLAB], ALU.mult),
                        rd=[qscB_l, cB["cqb"]], wr=[rsB[0]])
                    P.op("dve", lambda e, t0=t0: e.tensor_tensor(
                        rs_s[1][:], rops[:], cs["sqb"][:, t0:t0 + SLAB], ALU.mult),
                        rd=[ropsB, cB["sqb"]], wr=[rsB[1]])
                    P.op("dve", lambda e, qp_=qp_: e.tensor_tensor(
                        qro_s[qp_][:], rs_s[0][:], rs_s[1][:], ALU.add),
                        rd=[rsB[0], rsB[1]], wr=[qroB[qp_]])
                    P.store("act", q_d[dc * 128:(dc + 1) * 128, t0:t0 + SLAB],
                            qro_s[qp_][:], qroB[qp_], dB["q"])

                for dc in range(H):
                    dp = dc % 2
                    if s == 0:
                        P.dma("sp", wqf_s[dp][:].rearrange("p (c d) -> p c d", c=NCC),
                              wq_d.rearrange("(c p) d -> p c d", p=128)[:, :, dc * 128:(dc + 1) * 128],
                              wqfB[dp])
                        P.op("dve", lambda e, dp=dp: e.tensor_copy(wqb_s[dp][:], wqf_s[dp][:]),
                             rd=[wqfB[dp]], wr=[wqbB[dp]])
                        P.store("act", wqb_sd[:, dc * C:(dc + 1) * C],
                                wqb_s[dp][:], wqbB[dp], dB["wqb"])
                    else:
                        P.dma("sp", wqb_s[dp][:], wqb_sd[:, dc * C:(dc + 1) * C],
                              wqbB[dp], waits=dB["wqb"].dep_read())

                    def qmm(e, dp=dp, p=p):
                        last = None
                        for cc in range(NCC):
                            for j in range(NQM):
                                last = e.matmul(
                                    qps[dp][:, j * QW:(j + 1) * QW],
                                    wqb_s[dp][:, cc * 128:(cc + 1) * 128],
                                    xnT_s[p][:, cc * SLAB + j * QW: cc * SLAB + (j + 1) * QW],
                                    start=(cc == 0), stop=(cc == NCC - 1))
                        return last
                    P.op("pe", qmm, rd=[wqbB[dp], xnTB[p]], wr=[qpsB[dp]])
                    P.op("act", lambda e, dp=dp: e.copy(qsc_s2[dp][:], qps[dp][:]),
                         rd=[qpsB[dp]], wr=[qscB2[dp]])
                    f0_pump(2)
                    if dc >= 1:
                        rope_tail(dc - 1, t0)
                rope_tail(H - 1, t0)
                for which in (0, 1):
                    def kvmm(e, which=which, p=p):
                        last = None
                        for cc in range(NCC):
                            for j in range(NQM):
                                last = e.matmul(
                                    kvps[:, j * QW:(j + 1) * QW],
                                    wkvb_s[:, which * NCC * DL + cc * DL:
                                           which * NCC * DL + (cc + 1) * DL],
                                    xnT_s[p][:, cc * SLAB + j * QW: cc * SLAB + (j + 1) * QW],
                                    start=(cc == 0), stop=(cc == NCC - 1))
                        return last
                    P.op("pe", kvmm, rd=[wkvbB, xnTB[p]], wr=[kvpsB])
                    if which == 1:
                        P.op("act", lambda e, t0=t0: e.copy(
                            vlat_s[:, t0:t0 + SLAB], kvps[:]),
                            rd=[kvpsB], wr=[vlatB], fresh=(s == 0))
                    else:
                        P.op("act", lambda e: e.copy(kvsc_s[:], kvps[:]),
                             rd=[kvpsB], wr=[kvscB])
                        P.op("pe", lambda e: e.matmul(rops[0:DL, :], cs["rotk"][:],
                                                      kvsc_s[:], start=True, stop=True),
                             rd=[kvscB, cB["rotk"]], wr=[ropsB])
                        P.op("dve", lambda e, t0=t0: e.tensor_tensor(
                            rs_s[0][0:DL, :], kvsc_s[:], cs["ckb"][:, t0:t0 + SLAB],
                            ALU.mult), rd=[kvscB, cB["ckb"]], wr=[rsB[0]])
                        P.op("dve", lambda e, t0=t0: e.tensor_tensor(
                            rs_s[1][0:DL, :], rops[0:DL, :], cs["skb"][:, t0:t0 + SLAB],
                            ALU.mult), rd=[ropsB, cB["skb"]], wr=[rsB[1]])
                        P.op("dve", lambda e, t0=t0: e.tensor_tensor(
                            klat_s[:, t0:t0 + SLAB], rs_s[0][0:DL, :], rs_s[1][0:DL, :],
                            ALU.add), rd=[rsB[0], rsB[1]], wr=[klatB], fresh=(s == 0))

        f1s = contextlib.ExitStack()
        stack.enter_context(f1s)
        TQ = T // 4
        NMM = TQ // 512 if TQ >= 512 else 1
        MW = min(512, TQ)
        xf_s = mksb(f1s, "xfq", [128, NCC * TQ], BF16)
        guw_s = [mksb(f1s, f"guw{i}", [128, NCC * 256], BF16) for i in range(2)]
        sil_s = mksb(f1s, "sil", [128, TQ], BF16)
        hb_s = [mksb(f1s, f"hb{i}", [128, TQ], BF16) for i in range(2)]
        gps = mkps(f1s, "gp0", [128, TQ])
        ups = mkps(f1s, "up0", [128, TQ])
        xfB = Buf("xfq")
        guwB = [Buf(f"guw{i}") for i in range(2)]
        silB = Buf("sil")
        hbB = [Buf(f"hb{i}") for i in range(2)]
        gpsB = Buf("gp0")
        upsB = Buf("up0")

        def f1_load_xf(q):
            for g4 in range(4):
                P.dma("sp", xf_s[:].rearrange("p (c t) -> p c t", c=NCC)[:, g4::4, :],
                      xn_d.rearrange("(c p) t -> p c t", p=128)[:, g4::4, q * TQ:(q + 1) * TQ],
                      xfB, fresh=(g4 == 0), waits=dB["xn"].dep_read())

        def f1_load_guw(oc):
            op_ = oc % 2
            for g4 in range(4):
                P.dma("sp",
                      guw_s[op_][:].rearrange("p (c d) -> p c d", c=NCC)[:, g4::4, 0:128],
                      wgu_sd.rearrange("(c p) o -> p c o", p=128)[:, g4::4, oc * 128:(oc + 1) * 128],
                      guwB[op_], fresh=(g4 == 0), waits=dB["wgu"].dep_read())
                P.dma("sp",
                      guw_s[op_][:].rearrange("p (c d) -> p c d", c=NCC)[:, g4::4, 128:256],
                      wgu_sd.rearrange("(c p) o -> p c o", p=128)[:, g4::4, I + oc * 128:I + (oc + 1) * 128],
                      guwB[op_], fresh=False, waits=dB["wgu"].dep_read())

        def mk_f1_unit(q, oc):
            op_ = oc % 2

            def work():
                def gumm(e, gate):
                    last = None
                    col = 0 if gate else 128
                    tgt = gps if gate else ups
                    for cc in range(NCC):
                        for m in range(NMM):
                            last = e.matmul(
                                tgt[:, m * MW:(m + 1) * MW],
                                guw_s[op_][:, cc * 256 + col: cc * 256 + col + 128],
                                xf_s[:, cc * TQ + m * MW: cc * TQ + (m + 1) * MW],
                                start=(cc == 0), stop=(cc == NCC - 1))
                    return last
                P.op("pe", lambda e: gumm(e, True), rd=[guwB[op_], xfB], wr=[gpsB])
                P.op("pe", lambda e: gumm(e, False), rd=[guwB[op_], xfB], wr=[upsB])
                P.op("act", lambda e: e.activation(sil_s[:], gps[:], AF.Silu),
                     rd=[gpsB], wr=[silB])
                jp = oc % 2
                P.op("dve", lambda e, jp=jp: e.tensor_tensor(
                    hb_s[jp][:], sil_s[:], ups[:], ALU.mult),
                    rd=[silB, upsB], wr=[hbB[jp]])
                P.store("sp", h_sd[oc * 128:(oc + 1) * 128, q * TQ:(q + 1) * TQ],
                        hb_s[jp][:], hbB[jp], dB["h"])
            return (q, oc, work)

        f1_units = []
        if upto >= 4:
            for q in range(4):
                for oc in range(NOC):
                    f1_units.append(mk_f1_unit(q, oc))
        f1_state = {"i": 0}

        def f1_pump(k):
            for _ in range(k):
                i = f1_state["i"]
                if i >= len(f1_units):
                    return
                q, oc = f1_units[i][0], f1_units[i][1]
                if oc == 0:
                    f1_load_xf(q)
                    f1_load_guw(0)
                if oc + 1 < NOC:
                    f1_load_guw(oc + 1)
                f1_units[i][2]()
                f1_state["i"] = i + 1

        P.barrier(include_pool=False)

        a2 = contextlib.ExitStack()
        with a2:
            KCH = T // 128
            keb_s = mksb(a2, "keb", [DL, H * DH], BF16)
            veb_s = mksb(a2, "veb", [DL, H * DH], BF16)
            qh_s = [mksb(a2, f"qh{i}", [128, T], BF16) for i in range(2)]
            ktb_s = mksb(a2, "ktb", [128, T], BF16)
            vtb_s = mksb(a2, "vtb", [128, T], BF16)
            pt_s = [mksb(a2, f"pt{i}", [128, QT], BF16) for i in range(4)]
            pd_s = mksb(a2, "pd", [128, 128], BF16)
            lsum_s = mksb(a2, "lsum", [1, QT], F32)
            rsum_s = mksb(a2, "rsum", [1, QT], F32)
            rbcs_s = mksb(a2, "rbcs", [128, QT], F32)
            ytb_s = mksb(a2, "ytb", [128, T], BF16)

            sps = [mkps(a2, f"sp{i}", [128, QT]) for i in range(2)]
            yps = mkps(a2, "yps", [128, QT])
            smp = mkps(a2, "smp", [1, QT])

            kebB, vebB = Buf("keb"), Buf("veb")
            qhB = [Buf(f"qh{i}") for i in range(2)]
            ktbB, vtbB = Buf("ktb"), Buf("vtb")
            ptB = [Buf(f"pt{i}") for i in range(4)]
            pdB = Buf("pd")
            lsumB, rsumB, rbcsB = Buf("lsum"), Buf("rsum"), Buf("rbcs")
            ytbB = Buf("ytb")
            spsB = [Buf(f"sp{i}") for i in range(2)]
            ypsB, smpB = Buf("yps"), Buf("smp")

            with nc.sbuf_tensor("kvstage", [DL, H * DH], F32) as kvstage:
                kvsB = Buf("kvstage")
                P.dma("sp", kvstage[:].rearrange("p (h d) -> p h d", h=H),
                      ke_d.rearrange("(h p) d -> p h d", p=DL), kvsB)
                P.op("dve", lambda e: e.tensor_copy(keb_s[:], kvstage[:]),
                     rd=[kvsB], wr=[kebB])
                P.dma("sp", kvstage[:].rearrange("p (h d) -> p h d", h=H),
                      ve_d.rearrange("(h p) d -> p h d", p=DL), kvsB)
                P.op("dve", lambda e: e.tensor_copy(veb_s[:], kvstage[:]),
                     rd=[kvsB], wr=[vebB])

            for h in range(H if upto >= 2 else 0):
                hp = h % 2
                P.dma("sp", qh_s[hp][:], q_d[h * 128:(h + 1) * 128, :], qhB[hp],
                      waits=dB["q"].dep_read())
                for kb in range(T // 512):
                    kx = kb % 2
                    P.op("pe", lambda e, kb=kb, h=h, kx=kx: e.matmul(
                        sps[kx][:], keb_s[:, h * DH:(h + 1) * DH],
                        klat_s[:, kb * 512:(kb + 1) * 512], start=True, stop=True),
                        rd=[kebB, klatB], wr=[spsB[kx]])
                    P.op("act", lambda e, kb=kb, kx=kx: e.copy(
                        ktb_s[:, kb * 512:(kb + 1) * 512], sps[kx][:]),
                        rd=[spsB[kx]], wr=[ktbB], fresh=(kb == 0))
                for kc in range(KCH):
                    kx = kc % 2
                    P.op("pe", lambda e, kc=kc, h=h, kx=kx: e.matmul(
                        sps[kx][:, 0:128], vlat_s[:, kc * 128:(kc + 1) * 128],
                        veb_s[:, h * DH:(h + 1) * DH], start=True, stop=True),
                        rd=[vebB, vlatB], wr=[spsB[kx]])
                    P.op("dve", lambda e, kc=kc, kx=kx: e.tensor_copy(
                        vtb_s[:, kc * 128:(kc + 1) * 128], sps[kx][:, 0:128]),
                        rd=[spsB[kx]], wr=[vtbB], fresh=(kc == 0))
                f1_pump(1)
                for qt in range(NQT):
                    kmax = (qt * QT + QT) // 128

                    def scores(kc, qt=qt, hp=hp):
                        a = kc * 128 - qt * QT
                        q0 = max(0, a)
                        sp_ = kc % 2
                        pp = kc % 4
                        P.op("pe", lambda e, sp_=sp_, kc=kc, qt=qt, q0=q0, hp=hp: e.matmul(
                            sps[sp_][:, q0:QT], ktb_s[:, kc * 128:(kc + 1) * 128],
                            qh_s[hp][:, qt * QT + q0: qt * QT + QT],
                            start=True, stop=True), rd=[ktbB, qhB[hp]], wr=[spsB[sp_]])
                        if a >= 0:
                            P.op("act", lambda e, sp_=sp_, q0=q0: e.activation(
                                pd_s[:], sps[sp_][:, q0:q0 + 128], AF.Exp,
                                bias=cs["nbias"][:, 0:1], scale=SCALE),
                                rd=[spsB[sp_], cB["nbias"]], wr=[pdB])
                            P.op("dve", lambda e, pp=pp, q0=q0: e.tensor_tensor(
                                pt_s[pp][:, q0:q0 + 128], pd_s[:], cs["tri"][:], ALU.mult),
                                rd=[pdB, cB["tri"]], wr=[ptB[pp]])
                            if QT - q0 > 128:
                                P.op("act", lambda e, sp_=sp_, pp=pp, q0=q0: e.activation(
                                    pt_s[pp][:, q0 + 128:QT], sps[sp_][:, q0 + 128:QT],
                                    AF.Exp, bias=cs["nbias"][:, 0:1], scale=SCALE),
                                    rd=[spsB[sp_], cB["nbias"]], wr=[ptB[pp]], fresh=False)
                        else:
                            P.op("act", lambda e, sp_=sp_, pp=pp: e.activation(
                                pt_s[pp][:], sps[sp_][:], AF.Exp,
                                bias=cs["nbias"][:, 0:1], scale=SCALE),
                                rd=[spsB[sp_], cB["nbias"]], wr=[ptB[pp]])

                    def sums_pv_k(kc, qt=qt, kmax=kmax):
                        a = kc * 128 - qt * QT
                        q0 = max(0, a)
                        pp = kc % 4

                        def sums_pv(e, pp=pp, kc=kc, q0=q0, kmax=kmax):
                            e.matmul(smp[:, q0:QT], cs["onesb"][:], pt_s[pp][:, q0:QT],
                                     start=(kc == 0), stop=(kc == kmax - 1))
                            return e.matmul(yps[:, q0:QT],
                                            vtb_s[:, kc * 128:(kc + 1) * 128],
                                            pt_s[pp][:, q0:QT],
                                            start=(kc == 0), stop=(kc == kmax - 1))
                        P.op("pe", sums_pv, rd=[ptB[pp], cB["onesb"], vtbB],
                             wr=[smpB, ypsB], fresh=(kc == 0))

                    scores(0)
                    for kc in range(kmax):
                        if kc + 1 < kmax:
                            scores(kc + 1)
                        sums_pv_k(kc)
                    f0_pump(2)
                    f1_pump(1)
                    P.op("act", lambda e: e.activation(lsum_s[:], smp[:], AF.Ln),
                         rd=[smpB], wr=[lsumB])
                    P.op("act", lambda e: e.activation(rsum_s[:], lsum_s[:], AF.Exp,
                                                       scale=-1.0),
                         rd=[lsumB], wr=[rsumB])
                    P.op("pe", lambda e: e.matmul(sps[0][:], cs["onesf"][:], rsum_s[:],
                                                  start=True, stop=True),
                         rd=[rsumB, cB["onesf"]], wr=[spsB[0]])
                    P.op("dve", lambda e: e.tensor_copy(rbcs_s[:], sps[0][:]),
                         rd=[spsB[0]], wr=[rbcsB])
                    P.op("dve", lambda e, qt=qt: e.tensor_tensor(
                        ytb_s[:, qt * QT:(qt + 1) * QT], yps[:], rbcs_s[:], ALU.mult),
                        rd=[ypsB, rbcsB], wr=[ytbB], fresh=(qt == 0))
                P.store("act", y_d[h * 128:(h + 1) * 128, :], ytb_s[:], ytbB, dB["y"])

        f0_pump(len(f0_units))
        P.barrier(include_pool=True)

        a3 = contextlib.ExitStack()
        with a3:
            pwf_s = mksb(a3, "pwf0", [128, C], F32)
            pwb_s = mksb(a3, "pwb", [128, NCC * C], BF16)
            ysl_s = mksb(a3, "ysl0", [128, NCC * 512], BF16)
            asb_s = mksb(a3, "asb0", [128, C], F32)
            aps = [mkps(a3, f"ap{i}", [128, 1024]) for i in range(2)]

            pwfB = Buf("pwf0")
            pwbB = Buf("pwb")
            yslB = Buf("ysl0")
            asbB = Buf("asb0")
            apsB = [Buf(f"ap{i}") for i in range(2)]

            for hd in range(NCC):
                P.dma("sp", pwf_s[:], pw_d[hd * 128:(hd + 1) * 128, :], pwfB)
                P.op("dve", lambda e, hd=hd: e.tensor_copy(
                    pwb_s[:, hd * C:(hd + 1) * C], pwf_s[:]),
                    rd=[pwfB], wr=[pwbB], fresh=(hd == 0))
            for qg in range(T // 512 if upto >= 3 else 0):
                for g4 in range(4):
                    P.dma("sp",
                          ysl_s[:].rearrange("p (c t) -> p c t", c=NCC)[:, g4::4, :],
                          y_d.rearrange("(c p) t -> p c t", p=128)[:, g4::4, qg * 512:(qg + 1) * 512],
                          yslB, fresh=(g4 == 0), waits=dB["y"].dep_read())
                for jb in range(4):
                    qb = qg * 4 + jb

                    def pmm(e, jb=jb):
                        last = None
                        for ccs in range(4):
                            for hd in range(NCC):
                                last = e.matmul(
                                    aps[ccs // 2][:, (ccs % 2) * 512:(ccs % 2) * 512 + 512],
                                    ysl_s[:, hd * 512 + jb * 128: hd * 512 + jb * 128 + 128],
                                    pwb_s[:, hd * C + ccs * 512: hd * C + ccs * 512 + 512],
                                    start=(hd == 0), stop=(hd == NCC - 1))
                        return last
                    P.op("pe", pmm, rd=[yslB, pwbB], wr=[apsB[0], apsB[1]])
                    f1_pump(3)
                    P.op("act", lambda e: e.copy(asb_s[:, 0:1024], aps[0][:]),
                         rd=[apsB[0]], wr=[asbB])
                    P.op("act", lambda e: e.copy(asb_s[:, 1024:2048], aps[1][:]),
                         rd=[apsB[1]], wr=[asbB], fresh=False)
                    P.store("sp", attn_d[qb * 128:(qb + 1) * 128, :], asb_s[:],
                            asbB, dB["attn"])

        f1_pump(len(f1_units))
        f1s.close()
        f0s.close()
        acm.close()
        P.barrier(include_pool=True)

        c1 = contextlib.ExitStack()
        with c1:
            NDM = TT // 512 if TT >= 512 else 1
            DW = min(512, TT)
            NTB = TT // 128
            ht_s = [mksb(c1, f"ht{i}", [128, NOC * TT], BF16) for i in range(2)]
            dnw_s = [mksb(c1, f"dnw{i}", [128, NOC * 256], BF16) for i in range(2)]
            ybf_s = [mksb(c1, f"ybf{i}", [128, NCC * TT], BF16) for i in range(2)]
            y2_s = [mksb(c1, f"y2{i}", [128, TT], BF16) for i in range(2)]
            lr_s = mksb(c1, "lr", [1, TT], F32)
            rr_s = mksb(c1, "rr", [1, TT], F32)
            rcol_s = [mksb(c1, f"rcol{i}", [128, NTB], F32) for i in range(2)]
            xa_s = [mksb(c1, f"xa{i}", [128, C], F32) for i in range(2)]
            ob_s = [mksb(c1, f"ob{i}", [128, C], F32) for i in range(2)]
            yds = [mkps(c1, f"yd{i}", [128, TT]) for i in range(2)]
            smq2 = [mkps(c1, f"smq{i}", [1, TT]) for i in range(2)]
            tps2 = [mkps(c1, f"tq{i}", [128, 128], BF16) for i in range(2)]
            rcp = mkps(c1, "rcp", [128, 1])

            htB = [Buf(f"ht{i}") for i in range(2)]
            dnwB = [Buf(f"dnw{i}") for i in range(2)]
            ybfB = [Buf(f"ybf{i}") for i in range(2)]
            y2B = [Buf(f"y2{i}") for i in range(2)]
            lrB, rrB = Buf("lr"), Buf("rr")
            rcolB = [Buf(f"rcol{i}") for i in range(2)]
            xaB = [Buf(f"xa{i}") for i in range(2)]
            xcB = [Buf(f"xc{i}") for i in range(2)]
            obB = [Buf(f"ob{i}") for i in range(2)]
            ydsB = [Buf(f"yd{i}") for i in range(2)]
            smqB2 = [Buf(f"smq{i}") for i in range(2)]
            tpsB = [Buf(f"tq{i}") for i in range(2)]
            rcpB = Buf("rcp")

            pend_rstd = []
            pend_asm = []
            pend_sq = []
            for tt in range(NTT if upto >= 5 else 0):
                ttp = tt % 2
                tc0 = tt * TT
                for g4 in range(4):
                    P.dma("sp", ht_s[ttp][:].rearrange("p (c t) -> p c t", c=NOC)[:, g4::4, :],
                          h_sd.rearrange("(c p) t -> p c t", p=128)[:, g4::4, tc0:tc0 + TT],
                          htB[ttp], fresh=(g4 == 0), waits=dB["h"].dep_read())

                def rstd_cl(tt=tt, ttp=ttp):
                    P.op("act", lambda e, ttp=ttp: e.activation(
                        lr_s[:], smq2[ttp][:], AF.Ln,
                        bias=cs["epsr"][0:1, 0:1], scale=1.0 / C),
                        rd=[smqB2[ttp], cB["epsr"]], wr=[lrB])
                    P.op("act", lambda e: e.activation(rr_s[:], lr_s[:], AF.Exp, scale=-0.5),
                         rd=[lrB], wr=[rrB])
                    for tb in range(NTB):
                        P.op("pe", lambda e, tb=tb: e.matmul(
                            rcp[:], rr_s[:, tb * 128:(tb + 1) * 128], cs["onef1"][:],
                            start=True, stop=True), rd=[rrB, cB["onef1"]], wr=[rcpB])
                        P.op("dve", lambda e, tb=tb, ttp=ttp: e.tensor_copy(
                            rcol_s[ttp][:, tb:tb + 1], rcp[:]), rd=[rcpB],
                            wr=[rcolB[ttp]], fresh=(tb == 0))

                def asm_cl(tb, tt=tt, ttp=ttp):
                    def run():
                        bp = tb % 2
                        rows = tt * TT + tb * 128
                        evx = P.dma("sp", xa_s[bp][:], xin[rows:rows + 128, :], xaB[bp])
                        P.dma("pool", xa_s[bp][:], attn_d[rows:rows + 128, :], xcB[bp],
                              waits=dB["attn"].dep_read() + [evx], accum=True)
                        for cc in range(NCC):
                            tp = cc % 2
                            P.op("pe", lambda e, tp=tp, cc=cc, tb=tb, ttp=ttp: e.transpose(
                                tps2[tp][:],
                                ybf_s[ttp][:, cc * TT + tb * 128: cc * TT + tb * 128 + 128],
                                cs["ident"][:]), rd=[ybfB[ttp], cB["ident"]], wr=[tpsB[tp]])
                            P.op("dve", lambda e, tp=tp, cc=cc, bp=bp, tb=tb, ttp=ttp:
                                 e.scalar_tensor_tensor(
                                     ob_s[bp][:, cc * 128:(cc + 1) * 128], tps2[tp][:],
                                     rcol_s[ttp][:, tb:tb + 1],
                                     xa_s[bp][:, cc * 128:(cc + 1) * 128],
                                     ALU.mult, ALU.add),
                                 rd=[tpsB[tp], rcolB[ttp], xaB[bp], xcB[bp]],
                                 wr=[obB[bp]], fresh=(cc == 0))
                        P.store("sp", out_d[rows:rows + 128, :], ob_s[bp][:], obB[bp], outB)
                    return run

                def dnw_load(pair, tt=tt):
                    pp_ = pair % 2
                    for g4 in range(4):
                        P.dma("act",
                              dnw_s[pp_][:].rearrange("p (c d) -> p c d", c=NOC)[:, g4::4, :],
                              wdn_sd.rearrange("(c p) o -> p c o", p=128)[:, g4::4, pair * 256:(pair + 1) * 256],
                              dnwB[pp_], fresh=(g4 == 0), waits=dB["wdn"].dep_read())

                for cc in range(NCC):
                    ccp = (cc // 2) % 2
                    ydp = cc % 2
                    if cc % 2 == 0:
                        if tt == 0 and cc == 0:
                            dnw_load(0)
                            dnw_load(1)
                        elif cc // 2 + 1 < NCC // 2:
                            dnw_load(cc // 2 + 1)
                        elif tt + 1 < NTT:
                            dnw_load(0)


                    def dmm(e, ccp=ccp, ydp=ydp, cc=cc, ttp=ttp):
                        last = None
                        half = (cc % 2) * 128
                        for ic in range(NOC):
                            for m in range(NDM):
                                last = e.matmul(
                                    yds[ydp][:, m * DW:(m + 1) * DW],
                                    dnw_s[ccp][:, ic * 256 + half: ic * 256 + half + 128],
                                    ht_s[ttp][:, ic * TT + m * DW:(ic) * TT + (m + 1) * DW],
                                    start=(ic == 0), stop=(ic == NOC - 1))
                        return last
                    P.op("pe", dmm, rd=[dnwB[ccp], htB[ttp]], wr=[ydsB[ydp]])
                    if cc == 1 and pend_rstd:
                        pend_rstd.pop()()
                    if cc >= 4 and cc % 3 == 1 and pend_asm:
                        pend_asm.pop(0)()
                    P.op("dve", lambda e, cc=cc, ydp=ydp, ttp=ttp: e.tensor_copy(
                        ybf_s[ttp][:, cc * TT:(cc + 1) * TT], yds[ydp][:]),
                        rd=[ydsB[ydp]], wr=[ybfB[ttp]], fresh=(cc == 0))
                    P.op("act", lambda e, cc=cc, ttp=ttp: e.activation(
                        y2_s[cc % 2][:], ybf_s[ttp][:, cc * TT:(cc + 1) * TT], AF.Square),
                        rd=[ybfB[ttp]], wr=[y2B[cc % 2]])

                    def sq_cl(cc=cc, ttp=ttp, y2e=None):
                        def sqmm(e, cc=cc, ttp=ttp):
                            last = None
                            for m in range(NDM):
                                last = e.matmul(smq2[ttp][:, m * DW:(m + 1) * DW],
                                                cs["onesb"][:],
                                                y2_s[cc % 2][:, m * DW:(m + 1) * DW],
                                                start=(cc == 0), stop=(cc == NCC - 1))
                            return last
                        P.op("pe", sqmm, rd=[y2B[cc % 2], cB["onesb"]], wr=[smqB2[ttp]],
                             fresh=(cc == 0))
                    pend_sq.append(sq_cl)
                    if len(pend_sq) > 1:
                        pend_sq.pop(0)()
                while pend_sq:
                    pend_sq.pop(0)()
                pend_rstd.append(rstd_cl)
                for tb in range(NTB):
                    pend_asm.append(asm_cl(tb))
            while pend_rstd:
                pend_rstd.pop()()
            while pend_asm:
                pend_asm.pop(0)()

        P.final_wait(outB)

        with nc.Block() as block:
            P.emit(block)
    return nc


def kernel(**inputs):
    cfg = Cfg()
    prep = host_prep(inputs, cfg)
    x = np.ascontiguousarray(np.asarray(inputs["L0_attn_x"], np.float32))
    nc = build_program(cfg)
    in_maps = []
    for b in range(8):
        m = {"x": np.ascontiguousarray(x[b]),
             "wq": prep["wq"], "wk": prep["wk"], "wv": prep["wv"],
             "ke": prep["ke"], "ve": prep["ve"], "pw": prep["pw"],
             "gub": prep["gub"], "gus": prep["gus"],
             "dnb": prep["dnb"], "dns": prep["dns"]}
        in_maps.append(m)
    res = run_bass_kernel_spmd(nc, in_maps, core_ids=list(range(8)))
    return np.stack([np.asarray(res.results[b]["out"]) for b in range(8)], axis=0)


# version 13
# speedup vs baseline: 1.1513x; 1.1513x over previous
import contextlib
import numpy as np
import ml_dtypes

import concourse.bass as bass
import concourse.mybir as mybir
from concourse.bass_utils import run_bass_kernel_spmd

F32 = mybir.dt.float32
BF16 = mybir.dt.bfloat16
F16 = mybir.dt.float16
U16 = mybir.dt.uint16
I32 = mybir.dt.int32
AF = mybir.ActivationFunctionType
ALU = mybir.AluOpType
AX = mybir.AxisListType
NPBF16 = ml_dtypes.bfloat16


class Cfg:
    def __init__(self, T=4096, I=5632):
        self.C = 2048
        self.T = T
        self.H = 16
        self.DH = 128
        self.DL = 64
        self.I = I
        self.NCC = self.C // 128
        self.SLAB = T // 8 if T >= 1024 else 128
        self.NSLAB = T // self.SLAB
        self.QT = 512
        self.NQT = T // 512
        self.TT = T // 8
        self.NTT = 8
        self.NOC = I // 128
        self.NBC_GU = self.C // 256
        self.NBC_DN = I // 256
        self.EXP_SHIFT = 12.0
        self.EPS_LN = 1e-5
        self.EPS_RMS = 1e-6
        assert T % 512 == 0 and self.SLAB % 128 == 0 and I % 256 == 0


def _perm_half(n):
    p = np.empty(n, np.int64)
    for f in range(n):
        p[(f & 1) * (n // 2) + (f >> 1)] = f
    return p


def host_prep(inputs, cfg):
    C, H, DH, DL, I = cfg.C, cfg.H, cfg.DH, cfg.DL, cfg.I
    fo_gu = _perm_half(C)
    fo_dn = _perm_half(I)
    do_h = _perm_half(DH)
    lo_l = _perm_half(DL)

    d = {}
    wq = np.asarray(inputs["L0_attn_wq"])
    col_perm = np.concatenate([h * DH + do_h for h in range(H)])
    d["wq"] = np.ascontiguousarray(wq[np.ix_(fo_gu, col_perm)], np.float32)
    d["wk"] = np.ascontiguousarray(
        np.asarray(inputs["L0_attn_wk_lat"])[np.ix_(fo_gu, lo_l)], np.float32)
    d["wv"] = np.ascontiguousarray(np.asarray(inputs["L0_attn_wv_lat"])[fo_gu], np.float32)
    ke = np.asarray(inputs["L0_attn_k_expand"])[:, lo_l][:, :, do_h]
    d["ke"] = np.ascontiguousarray(ke.reshape(H * DL, DH), np.float32)
    d["ve"] = np.ascontiguousarray(
        np.asarray(inputs["L0_attn_v_expand"]).reshape(H * DL, DH), np.float32)
    d["pw"] = np.ascontiguousarray(np.asarray(inputs["L0_attn_proj_w"]), np.float32)

    gup = np.asarray(inputs["L0_ffn_gate_up_packed"])
    gus = np.asarray(inputs["L0_ffn_gate_up_scales"])
    dnp_ = np.asarray(inputs["L0_ffn_down_packed"])
    dns = np.asarray(inputs["L0_ffn_down_scales"])

    def widen(b):
        w = np.zeros(b.shape + (2,), np.uint8)
        w[..., 0] = b
        return np.ascontiguousarray(w.view(np.uint16)[..., 0])

    gu_col = np.concatenate([fo_dn, I + fo_dn])
    gu_bytes = gup.view(np.uint8).reshape(2 * I, C // 2)
    d["gub"] = widen(np.ascontiguousarray(gu_bytes[gu_col].T))
    d["gus"] = np.ascontiguousarray(gus[:, gu_col])
    dn_bytes = dnp_.view(np.uint8).reshape(C, I // 2)
    d["dnb"] = widen(np.ascontiguousarray(dn_bytes.T))
    d["dns"] = np.ascontiguousarray(dns)
    return d


def rope_tables(t, dim):
    inv_freq = 1.0 / 10000.0 ** (np.arange(0, dim, 2, dtype=np.float32) / dim)
    ang = np.arange(t, dtype=np.float32)[:, None] * inv_freq[None, :]
    return (np.ascontiguousarray(np.cos(ang).T).astype(NPBF16),
            np.ascontiguousarray(np.sin(ang).T).astype(NPBF16))


def rot_mat_T(dim):
    half = dim // 2
    m = np.zeros((dim, dim), NPBF16)
    for i in range(half):
        m[i, half + i] = 1.0
        m[half + i, i] = -1.0
    return m


ENGS = ("pe", "act", "dve", "pool", "sp")


class Buf:
    __slots__ = ("name", "w", "r", "war", "dsem", "dcnt", "pool_dma")

    def __init__(self, name):
        self.name = name
        self.w = []
        self.r = []
        self.war = []
        self.dsem = None
        self.dcnt = 0
        self.pool_dma = False

    def dep_write(self, fresh):
        if fresh:
            self.war = self.w + self.r
            self.w = []
            self.r = []
        return list(self.war)

    def dep_read(self):
        return list(self.w)


class Prog:
    def __init__(self, nc, stack):
        self.nc = nc
        self.stack = stack
        self.sem = {e: stack.enter_context(nc.semaphore(f"S_{e}")) for e in ENGS}
        self.cnt = {e: 0 for e in ENGS}
        self.recs = {e: [] for e in ENGS}
        self.all_dma_bufs = []

    def op(self, eng, fn, rd=(), wr=(), fresh=True, waits=()):
        wts = list(waits)
        for b in rd:
            wts += b.dep_read()
        for b in wr:
            wts += b.dep_write(fresh)
        self.cnt[eng] += 1
        ev = (self.sem[eng], self.cnt[eng])
        self.recs[eng].append((wts, fn, ev, 1))
        for b in rd:
            b.r.append(ev)
        for b in wr:
            b.w.append(ev)
        return ev

    def dma(self, queue, out_ap, in_ap, dst, src=(), fresh=True, waits=(), dram_dst=None,
            accum=False):
        if dst.dsem is None:
            dst.dsem = self.stack.enter_context(self.nc.semaphore(f"D_{dst.name}"))
            self.all_dma_bufs.append(dst)
            if queue == "pool":
                dst.pool_dma = True
        wts = list(waits)
        for b in src:
            wts += b.dep_read()
        wts += dst.dep_write(fresh)
        if dram_dst is not None:
            wts += dram_dst.dep_write(False)
        dst.dcnt += 16
        ev = (dst.dsem, dst.dcnt)

        def fn(e, _o=out_ap, _i=in_ap, _a=accum):
            if _a:
                return e.dma_start(out=_o, in_=_i, accum_op=mybir.AluOpType.add)
            return e.dma_start(out=_o, in_=_i)

        self.recs[queue].append((wts, fn, ev, 16))
        for b in src:
            b.r.append(ev)
        dst.w.append(ev)
        if dram_dst is not None:
            dram_dst.w.append(ev)
        return ev

    def store(self, queue, out_ap, in_ap, src_buf, dram_dst):
        key = (id(src_buf), id(dram_dst))
        if not hasattr(self, "_stslots"):
            self._stslots = {}
        slot = self._stslots.get(key)
        if slot is None:
            slot = Buf("stsl_" + src_buf.name + "_" + dram_dst.name)
            self._stslots[key] = slot
        return self.dma(queue, out_ap, in_ap, slot, src=[src_buf], fresh=False,
                        dram_dst=dram_dst)

    def barrier(self, include_pool=True):
        engs = [e for e in ENGS if include_pool or e != "pool"]
        wts = [(self.sem[e], self.cnt[e]) for e in engs if self.cnt[e] > 0]
        wts += [(b.dsem, b.dcnt) for b in self.all_dma_bufs
                if include_pool or not b.pool_dma]
        self.cnt["sp"] += 1
        ev = (self.sem["sp"], self.cnt["sp"])
        self.recs["sp"].append((wts, lambda e: e.nop(), ev, 1))
        for e in engs:
            if e == "sp":
                continue
            self.cnt[e] += 1
            self.recs[e].append(([ev], lambda en: en.nop(), (self.sem[e], self.cnt[e]), 1))

    def final_wait(self, buf):
        self.recs["sp"].append((buf.dep_read(), None, None, 0))

    def emit(self, block):
        hmap = {"pe": block.tensor, "act": block.scalar, "dve": block.vector,
                "pool": block.gpsimd, "sp": block.sync}

        def mk(eng):
            recs = self.recs[eng]

            def f(e):
                seen = {}
                for wts, fn, ev, inc in recs:
                    byname = {}
                    for (sem, val) in wts:
                        k = id(sem)
                        if byname.get(k, (None, -1))[1] < val:
                            byname[k] = (sem, val)
                    for k, (sem, val) in byname.items():
                        if seen.get(k, -1) >= val:
                            continue
                        e.wait_ge(sem, val)
                        seen[k] = val
                    if fn is None:
                        continue
                    inst = fn(e)
                    if inc:
                        inst.then_inc(ev[0], inc)
            return f

        for eng in ENGS:
            hmap[eng](mk(eng))


def build_program(cfg, dbg=False, upto=9, f2n=9):
    nc = bass.Bass("TRN2", target_bir_lowering=False)
    C, T, H, DH, DL, I = cfg.C, cfg.T, cfg.H, cfg.DH, cfg.DL, cfg.I
    NCC, SLAB, NSLAB = cfg.NCC, cfg.SLAB, cfg.NSLAB
    QT, NQT, TT, NTT, NOC = cfg.QT, cfg.NQT, cfg.TT, cfg.NTT, cfg.NOC
    SCALE = float(1.0 / np.sqrt(DH))

    xin = nc.dram_tensor("x", [T, C], F32, kind="ExternalInput").ap()
    wq_d = nc.dram_tensor("wq", [C, C], F32, kind="ExternalInput").ap()
    wk_d = nc.dram_tensor("wk", [C, DL], F32, kind="ExternalInput").ap()
    wv_d = nc.dram_tensor("wv", [C, DL], F32, kind="ExternalInput").ap()
    ke_d = nc.dram_tensor("ke", [H * DL, DH], F32, kind="ExternalInput").ap()
    ve_d = nc.dram_tensor("ve", [H * DL, DH], F32, kind="ExternalInput").ap()
    pw_d = nc.dram_tensor("pw", [C, C], F32, kind="ExternalInput").ap()
    gub_d = nc.dram_tensor("gub", [C // 2, 2 * I], U16, kind="ExternalInput").ap()
    gus_d = nc.dram_tensor("gus", [C // 128, 2 * I], F16, kind="ExternalInput").ap()
    dnb_d = nc.dram_tensor("dnb", [I // 2, C], U16, kind="ExternalInput").ap()
    dns_d = nc.dram_tensor("dns", [I // 128, C], F16, kind="ExternalInput").ap()
    out_d = nc.dram_tensor("out", [T, C], F32, kind="ExternalOutput").ap()

    sk = "ExternalOutput" if dbg else "Internal"
    xn_d = nc.dram_tensor("xn_s", [C, T], BF16, kind=sk).ap()
    q_d = nc.dram_tensor("q_s", [C, T], BF16, kind=sk).ap()
    y_d = nc.dram_tensor("y_s", [C, T], BF16, kind=sk).ap()
    attn_d = nc.dram_tensor("attn_s", [T, C], F32, kind=sk).ap()
    wgu_sd = nc.dram_tensor("wgu_s", [C, 2 * I], BF16, kind=sk).ap()
    wdn_sd = nc.dram_tensor("wdn_s", [I, C], BF16, kind=sk).ap()
    h_sd = nc.dram_tensor("h_s", [I, T], BF16, kind=sk).ap()
    wqb_sd = nc.dram_tensor("wqb_s", [128, (C // 128) * C], BF16, kind="Internal").ap()

    cqn, sqn = rope_tables(T, DH)
    ckn, skn = rope_tables(T, DL)
    consts_np = {
        "ident": np.eye(128, dtype=NPBF16),
        "tri": np.triu(np.ones((128, 128), NPBF16)),
        "onesb": np.ones((128, 1), NPBF16),
        "onesf": np.ones((1, 128), np.float32),
        "onef1": np.ones((1, 1), np.float32),
        "cqb": np.vstack([cqn, cqn]), "sqb": np.vstack([sqn, sqn]),
        "ckb": np.vstack([ckn, ckn]), "skb": np.vstack([skn, skn]),
        "rotq": rot_mat_T(DH), "rotk": rot_mat_T(DL),
    }
    consts_d = {k: nc.inline_tensor(v, name=k + "_c") for k, v in consts_np.items()}

    stack = contextlib.ExitStack()
    with stack:
        P = Prog(nc, stack)

        def mksb(st_, name, shape, dt):
            return st_.enter_context(nc.sbuf_tensor(name, shape, dt))

        def mkps(st_, name, shape, dt=F32):
            return st_.enter_context(nc.psum_tensor(name, shape, dt))

        cs = {}
        cB = {}
        ACM_CONSTS = ("cqb", "sqb", "ckb", "skb", "tri", "rotq", "rotk")
        for k, v in consts_np.items():
            if k in ACM_CONSTS:
                continue
            cs[k] = mksb(stack, k, list(v.shape), BF16 if v.dtype == NPBF16 else F32)
            cB[k] = Buf(k)
            P.dma("sp", cs[k][:], consts_d[k].ap(), cB[k])
        for k, val in (("nbias", -cfg.EXP_SHIFT), ("epsb", cfg.EPS_LN), ("epsr", cfg.EPS_RMS)):
            cs[k] = mksb(stack, k, [128, 1], F32)
            cB[k] = Buf(k)
            P.op("dve", lambda e, k=k, val=val: e.memset(cs[k][:], val), wr=[cB[k]])

        dB = {k: Buf(k) for k in ("xn", "q", "y", "attn", "wgu", "wdn", "h", "wqb")}
        outB = Buf("out")

        acm = contextlib.ExitStack()
        stack.enter_context(acm)
        for k in ("tri",):
            v = consts_np[k]
            cs[k] = mksb(acm, k, list(v.shape), BF16 if v.dtype == NPBF16 else F32)
            cB[k] = Buf(k)
            P.dma("sp", cs[k][:], consts_d[k].ap(), cB[k])
        klat_s = mksb(acm, "klat", [DL, T], BF16)
        vlat_s = mksb(acm, "vlat", [DL, T], BF16)
        klatB, vlatB = Buf("klat"), Buf("vlat")

        OSL = 1024
        f0s = contextlib.ExitStack()
        stack.enter_context(f0s)
        f0t = {}
        for nm, dt in (("d1", U16), ("d2a", U16), ("d2b", U16), ("d3a", BF16),
                       ("d3b", BF16), ("d4", U16), ("d5a", BF16), ("d5b", BF16),
                       ("p1a", BF16), ("p1b", BF16), ("p2", BF16)):
            f0t[nm] = (mksb(f0s, "f0" + nm, [128, OSL], dt), Buf("f0" + nm))
        f0byt = [(mksb(f0s, f"f0byt{i}", [128, OSL], U16), Buf(f"f0byt{i}")) for i in range(2)]
        f0sm = [(mksb(f0s, f"f0sm{i}", [128, OSL], F16), Buf(f"f0sm{i}")) for i in range(2)]
        f0wo = [(mksb(f0s, f"f0wo{i}", [128, OSL], BF16), Buf(f"f0wo{i}")) for i in range(2)]

        f0_units = []

        def mk_f0_unit(bytes_d, scales_d, w_out_d, wB, ncols, bc, sl, hi, ui):
            c0 = sl * OSL
            n = min(ncols, c0 + OSL) - c0
            byt, bB = f0byt[(ui // 2) % 2]
            smt, sB = f0sm[(ui // 2) % 2]
            sh = 2 if hi else 6
            msk = 0x80 if hi else 0x8
            sgs = -0.015625 if hi else -0.25
            ab = "b" if (ui % 2) else "a"
            d1, b1 = f0t["d1"]
            d2, b2 = f0t["d2" + ab]
            d3, b3 = f0t["d3" + ab]
            d4, b4 = f0t["d4"]
            d5, b5 = f0t["d5" + ab]
            p1, pb1 = f0t["p1" + ab]
            p2, pb2 = f0t["p2"]
            wo, woB = f0wo[ui % 2]

            def load():
                if not hi:
                    P.dma("pool", byt[:, 0:n], bytes_d[bc * 128:(bc + 1) * 128, c0:c0 + n], bB)
                    P.dma("pool", smt[0:64, 0:n],
                          scales_d[2 * bc:2 * bc + 1, c0:c0 + n].broadcast_to((64, n)), sB)
                    P.dma("pool", smt[64:128, 0:n],
                          scales_d[2 * bc + 1:2 * bc + 2, c0:c0 + n].broadcast_to((64, n)),
                          sB, fresh=False)

            def work():
                orow = bc * 128 + hi * ((C if w_out_d is wgu_sd else I) // 2)
                nh = n // 2
                mask32 = 0x01C001C0
                msk32 = (msk << 16) | msk
                sgsh = 12 if msk == 0x8 else 8
                P.op("dve", lambda e, nh=nh: e.tensor_scalar(
                    d1[:, 0:n].bitcast(I32), byt[:, 0:n].bitcast(I32), sh, mask32,
                    ALU.logical_shift_left, ALU.bitwise_and), rd=[bB], wr=[b1])
                P.op("dve", lambda e: e.tensor_scalar(
                    d2[:, 0:n], d1[:, 0:n], 0x3F00, None, ALU.add), rd=[b1], wr=[b2])
                P.op("dve", lambda e: e.tensor_scalar(
                    d3[:, 0:n], d2[:, 0:n].bitcast(BF16), 2.0, 1.0,
                    ALU.mult, ALU.subtract), rd=[b2], wr=[b3])
                P.op("dve", lambda e, msk32=msk32: e.tensor_scalar(
                    d4[:, 0:n].bitcast(I32), byt[:, 0:n].bitcast(I32), msk32, None,
                    ALU.bitwise_and), rd=[bB], wr=[b4])
                P.op("dve", lambda e, sgsh=sgsh: e.tensor_scalar(
                    d5[:, 0:n].bitcast(I32), d4[:, 0:n].bitcast(I32), sgsh, 0x3F803F80,
                    ALU.logical_shift_left, ALU.bitwise_or), rd=[b4], wr=[b5])
                P.op("dve", lambda e: e.tensor_tensor(
                    p1[:, 0:n], d2[:, 0:n].bitcast(BF16), d3[:, 0:n], ALU.min),
                    rd=[b2, b3], wr=[pb1])
                P.op("pool", lambda e: e.tensor_tensor(
                    p2[:, 0:n], p1[:, 0:n], d5[:, 0:n], ALU.mult),
                    rd=[pb1, b5], wr=[pb2])
                P.op("pool", lambda e: e.tensor_tensor(
                    wo[:, 0:n], p2[:, 0:n], smt[:, 0:n], ALU.mult),
                    rd=[pb2, sB], wr=[woB])
                P.store("pool", w_out_d[orow:orow + 128, c0:c0 + n], wo[:, 0:n], woB, wB)
            return load, work

        ui = 0
        n_gate_units = [0]
        for (bytes_d, scales_d, w_out_d, wB, ncols, nbc) in (
                (gub_d, gus_d, wgu_sd, dB["wgu"], 2 * I, cfg.NBC_GU),
                (dnb_d, dns_d, wdn_sd, dB["wdn"], C, cfg.NBC_DN)):
            nsl = (ncols + OSL - 1) // OSL
            for bc in range(nbc):
                for sl in range(nsl):
                    for hi in (0, 1):
                        f0_units.append(mk_f0_unit(bytes_d, scales_d, w_out_d, wB,
                                                   ncols, bc, sl, hi, ui))
                        ui += 1
            if w_out_d is wgu_sd:
                n_gate_units[0] = ui

        f0_state = {"i": 0}

        def f0_pump(k):
            for _ in range(k):
                i = f0_state["i"]
                if i >= len(f0_units):
                    return
                if i == 0:
                    f0_units[0][0]()
                    f0_units[1][0]()
                if i + 2 < len(f0_units):
                    f0_units[i + 2][0]()
                f0_units[i][1]()
                f0_state["i"] = i + 1


        a1 = contextlib.ExitStack()
        with a1:
            for k in ("cqb", "sqb", "ckb", "skb", "rotq", "rotk"):
                v = consts_np[k]
                cs[k] = mksb(a1, k, list(v.shape), BF16 if v.dtype == NPBF16 else F32)
                cB[k] = Buf(k)
                P.dma("sp", cs[k][:], consts_d[k].ap(), cB[k])
            XT = SLAB // 128
            NQM = max(1, SLAB // 512)
            QW = min(512, SLAB)
            xt_s = [mksb(a1, f"xts{i}", [128, C], F32) for i in range(2)]
            sq_s2 = [mksb(a1, f"sqs{i}", [128, C], BF16) for i in range(2)]
            xnm_s = [mksb(a1, f"xnm{i}", [128, C], BF16) for i in range(2)]
            st_s2 = [mksb(a1, f"sts{i}", [128, 10], F32) for i in range(2)]
            xnT_s = [mksb(a1, f"xnT{i}", [128, NCC * SLAB], BF16) for i in range(2)]
            wqf_s = [mksb(a1, f"wqf{i}", [128, NCC * 128], F32) for i in range(2)]
            wqb_s = [mksb(a1, f"wqb{i}", [128, NCC * 128], BF16) for i in range(2)]
            qsc_s2 = [mksb(a1, f"qsc{i}", [128, SLAB], BF16) for i in range(2)]
            qro_s = [mksb(a1, f"qro{i}", [128, SLAB], BF16) for i in range(2)]
            rs_s = [mksb(a1, f"rs{i}", [128, SLAB], BF16) for i in range(2)]
            wkvf_s = mksb(a1, "wkvf", [128, NCC * 2 * DL], F32)
            wkvb_s = mksb(a1, "wkvb", [128, NCC * 2 * DL], BF16)
            kvsc_s = mksb(a1, "kvsc", [DL, SLAB], BF16)

            trps = [mkps(a1, f"trp{i}", [128, 128], BF16) for i in range(4)]
            qps = [mkps(a1, f"qp{i}", [128, SLAB]) for i in range(2)]
            kvps = mkps(a1, "kvp", [DL, SLAB])
            rops = mkps(a1, "rop", [128, SLAB])

            xtB = [Buf(f"xts{i}") for i in range(2)]
            sqB2 = [Buf(f"sqs{i}") for i in range(2)]
            xnmB = [Buf(f"xnm{i}") for i in range(2)]
            stB2 = [[Buf(f"st{j}_{i}") for i in range(10)] for j in range(2)]
            xnTB = [Buf(f"xnT{i}") for i in range(2)]
            wqfB = [Buf(f"wqf{i}") for i in range(2)]
            wqbB = [Buf(f"wqb{i}") for i in range(2)]
            qscB2 = [Buf(f"qsc{i}") for i in range(2)]
            qroB = [Buf(f"qro{i}") for i in range(2)]
            rsB = [Buf(f"rs{i}") for i in range(2)]
            wkvfB = Buf("wkvf")
            wkvbB = Buf("wkvb")
            kvscB = Buf("kvsc")
            trpB = [Buf(f"trp{i}") for i in range(4)]
            qpsB = [Buf(f"qp{i}") for i in range(2)]
            kvpsB = Buf("kvp")
            ropsB = Buf("rop")

            P.dma("sp", wkvf_s[:, 0:NCC * DL].rearrange("p (c d) -> p c d", c=NCC),
                  wk_d.rearrange("(c p) d -> p c d", p=128), wkvfB)
            P.dma("sp", wkvf_s[:, NCC * DL:2 * NCC * DL].rearrange("p (c d) -> p c d", c=NCC),
                  wv_d.rearrange("(c p) d -> p c d", p=128), wkvfB, fresh=False)
            P.op("dve", lambda e: e.tensor_copy(wkvb_s[:], wkvf_s[:]),
                 rd=[wkvfB], wr=[wkvbB])

            for s in range(NSLAB if upto >= 1 else 0):
                p = s % 2
                t0 = s * SLAB
                for xt in range(XT):
                    xp = xt % 2
                    rows = t0 + xt * 128
                    P.dma("sp", xt_s[xp][:], xin[rows:rows + 128, :], xtB[xp])
                    st_s = st_s2[xp]
                    stB = stB2[xp]
                    sq_l = sq_s2[xp]
                    sqB = sqB2[xp]
                    P.op("dve", lambda e, xp=xp, st_s=st_s: e.tensor_reduce(
                        st_s[:, 0:1], xt_s[xp][:], AX.X, ALU.add, negate=True),
                        rd=[xtB[xp]], wr=[stB[0]])
                    P.op("act", lambda e, xp=xp, st_s=st_s, sq_l=sq_l: e.activation(
                        sq_l[:], xt_s[xp][:], AF.Square, accum_out=st_s[:, 1:2]),
                        rd=[xtB[xp]], wr=[sqB, stB[1]])
                    P.op("dve", lambda e, st_s=st_s: e.tensor_scalar(
                        st_s[:, 2:3], st_s[:, 0:1], 1.0 / C, None, ALU.mult),
                        rd=[stB[0]], wr=[stB[2]])
                    P.op("dve", lambda e, st_s=st_s: e.tensor_scalar(
                        st_s[:, 3:4], st_s[:, 1:2], 1.0 / C, None, ALU.mult),
                        rd=[stB[1]], wr=[stB[3]])
                    P.op("dve", lambda e, st_s=st_s: e.tensor_scalar(
                        st_s[:, 4:5], st_s[:, 2:3], st_s[:, 2:3], None, ALU.mult),
                        rd=[stB[2]], wr=[stB[4]])
                    P.op("dve", lambda e, st_s=st_s: e.tensor_tensor(
                        st_s[:, 5:6], st_s[:, 3:4], st_s[:, 4:5], ALU.subtract),
                        rd=[stB[3], stB[4]], wr=[stB[5]])
                    P.op("act", lambda e, st_s=st_s: e.activation(
                        st_s[:, 6:7], st_s[:, 5:6], AF.Sqrt, bias=cs["epsb"][:, 0:1]),
                        rd=[stB[5], cB["epsb"]], wr=[stB[6]])
                    P.op("dve", lambda e, st_s=st_s: e.reciprocal(st_s[:, 7:8], st_s[:, 6:7]),
                         rd=[stB[6]], wr=[stB[7]])
                    P.op("dve", lambda e, xp=xp, st_s=st_s: e.tensor_scalar(
                        xnm_s[xp][:], xt_s[xp][:], st_s[:, 2:3], st_s[:, 7:8],
                        ALU.add, ALU.mult), rd=[xtB[xp], stB[2], stB[7]], wr=[xnmB[xp]])
                    f0_pump(2)
                    for cc in range(NCC):
                        tp = cc % 4
                        par = cc // (NCC // 2)
                        base = (cc % (NCC // 2)) * 256 + par
                        P.op("pe", lambda e, tp=tp, xp=xp, base=base: e.transpose(
                            trps[tp][:], xnm_s[xp][:, base:base + 255:2], cs["ident"][:]),
                            rd=[xnmB[xp], cB["ident"]], wr=[trpB[tp]])
                        if cc % 4 >= 2:
                            P.op("dve", lambda e, tp=tp, p=p, cc=cc, xt=xt: e.tensor_copy(
                                xnT_s[p][:, cc * SLAB + xt * 128: cc * SLAB + xt * 128 + 128],
                                trps[tp][:]), rd=[trpB[tp]], wr=[xnTB[p]],
                                fresh=(cc == 0 and xt == 0))
                        else:
                            P.op("act", lambda e, tp=tp, p=p, cc=cc, xt=xt: e.copy(
                                xnT_s[p][:, cc * SLAB + xt * 128: cc * SLAB + xt * 128 + 128],
                                trps[tp][:]), rd=[trpB[tp]], wr=[xnTB[p]],
                                fresh=(cc == 0 and xt == 0))
                for g4 in range(4):
                    P.store("act",
                            xn_d.rearrange("(c p) t -> p c t", p=128)[:, g4::4, t0:t0 + SLAB],
                            xnT_s[p][:].rearrange("p (c t) -> p c t", c=NCC)[:, g4::4, :],
                            xnTB[p], dB["xn"])
                def rope_tail(dc, t0):
                    qp_ = dc % 2
                    qsc_l, qscB_l = qsc_s2[qp_], qscB2[qp_]
                    P.op("pe", lambda e, qsc_l=qsc_l: e.matmul(
                        rops[:], cs["rotq"][:], qsc_l[:], start=True, stop=True),
                        rd=[qscB_l, cB["rotq"]], wr=[ropsB])
                    P.op("dve", lambda e, t0=t0, qsc_l=qsc_l: e.tensor_tensor(
                        rs_s[0][:], qsc_l[:], cs["cqb"][:, t0:t0 + SLAB], ALU.mult),
                        rd=[qscB_l, cB["cqb"]], wr=[rsB[0]])
                    P.op("dve", lambda e, t0=t0: e.tensor_tensor(
                        rs_s[1][:], rops[:], cs["sqb"][:, t0:t0 + SLAB], ALU.mult),
                        rd=[ropsB, cB["sqb"]], wr=[rsB[1]])
                    P.op("dve", lambda e, qp_=qp_: e.tensor_tensor(
                        qro_s[qp_][:], rs_s[0][:], rs_s[1][:], ALU.add),
                        rd=[rsB[0], rsB[1]], wr=[qroB[qp_]])
                    P.store("act", q_d[dc * 128:(dc + 1) * 128, t0:t0 + SLAB],
                            qro_s[qp_][:], qroB[qp_], dB["q"])

                for dc in range(H):
                    dp = dc % 2
                    if s == 0:
                        P.dma("sp", wqf_s[dp][:].rearrange("p (c d) -> p c d", c=NCC),
                              wq_d.rearrange("(c p) d -> p c d", p=128)[:, :, dc * 128:(dc + 1) * 128],
                              wqfB[dp])
                        P.op("dve", lambda e, dp=dp: e.tensor_copy(wqb_s[dp][:], wqf_s[dp][:]),
                             rd=[wqfB[dp]], wr=[wqbB[dp]])
                        P.store("act", wqb_sd[:, dc * C:(dc + 1) * C],
                                wqb_s[dp][:], wqbB[dp], dB["wqb"])
                    else:
                        P.dma("sp", wqb_s[dp][:], wqb_sd[:, dc * C:(dc + 1) * C],
                              wqbB[dp], waits=dB["wqb"].dep_read())

                    def qmm(e, dp=dp, p=p):
                        last = None
                        for cc in range(NCC):
                            for j in range(NQM):
                                last = e.matmul(
                                    qps[dp][:, j * QW:(j + 1) * QW],
                                    wqb_s[dp][:, cc * 128:(cc + 1) * 128],
                                    xnT_s[p][:, cc * SLAB + j * QW: cc * SLAB + (j + 1) * QW],
                                    start=(cc == 0), stop=(cc == NCC - 1))
                        return last
                    P.op("pe", qmm, rd=[wqbB[dp], xnTB[p]], wr=[qpsB[dp]])
                    P.op("act", lambda e, dp=dp: e.copy(qsc_s2[dp][:], qps[dp][:]),
                         rd=[qpsB[dp]], wr=[qscB2[dp]])
                    if dc >= 1:
                        rope_tail(dc - 1, t0)
                rope_tail(H - 1, t0)
                for which in (0, 1):
                    def kvmm(e, which=which, p=p):
                        last = None
                        for cc in range(NCC):
                            for j in range(NQM):
                                last = e.matmul(
                                    kvps[:, j * QW:(j + 1) * QW],
                                    wkvb_s[:, which * NCC * DL + cc * DL:
                                           which * NCC * DL + (cc + 1) * DL],
                                    xnT_s[p][:, cc * SLAB + j * QW: cc * SLAB + (j + 1) * QW],
                                    start=(cc == 0), stop=(cc == NCC - 1))
                        return last
                    P.op("pe", kvmm, rd=[wkvbB, xnTB[p]], wr=[kvpsB])
                    if which == 1:
                        P.op("act", lambda e, t0=t0: e.copy(
                            vlat_s[:, t0:t0 + SLAB], kvps[:]),
                            rd=[kvpsB], wr=[vlatB], fresh=(s == 0))
                    else:
                        P.op("act", lambda e: e.copy(kvsc_s[:], kvps[:]),
                             rd=[kvpsB], wr=[kvscB])
                        P.op("pe", lambda e: e.matmul(rops[0:DL, :], cs["rotk"][:],
                                                      kvsc_s[:], start=True, stop=True),
                             rd=[kvscB, cB["rotk"]], wr=[ropsB])
                        P.op("dve", lambda e, t0=t0: e.tensor_tensor(
                            rs_s[0][0:DL, :], kvsc_s[:], cs["ckb"][:, t0:t0 + SLAB],
                            ALU.mult), rd=[kvscB, cB["ckb"]], wr=[rsB[0]])
                        P.op("dve", lambda e, t0=t0: e.tensor_tensor(
                            rs_s[1][0:DL, :], rops[0:DL, :], cs["skb"][:, t0:t0 + SLAB],
                            ALU.mult), rd=[ropsB, cB["skb"]], wr=[rsB[1]])
                        P.op("dve", lambda e, t0=t0: e.tensor_tensor(
                            klat_s[:, t0:t0 + SLAB], rs_s[0][0:DL, :], rs_s[1][0:DL, :],
                            ALU.add), rd=[rsB[0], rsB[1]], wr=[klatB], fresh=(s == 0))

        f1s = contextlib.ExitStack()
        stack.enter_context(f1s)
        TQ = T // 4
        NMM = TQ // 512 if TQ >= 512 else 1
        MW = min(512, TQ)
        xf_s = mksb(f1s, "xfq", [128, NCC * TQ], BF16)
        guw_s = [mksb(f1s, f"guw{i}", [128, NCC * 256], BF16) for i in range(2)]
        sil_s = mksb(f1s, "sil", [128, TQ], BF16)
        hb_s = [mksb(f1s, f"hb{i}", [128, TQ], BF16) for i in range(2)]
        gps = mkps(f1s, "gp0", [128, TQ])
        ups = mkps(f1s, "up0", [128, TQ])
        xfB = Buf("xfq")
        guwB = [Buf(f"guw{i}") for i in range(2)]
        silB = Buf("sil")
        hbB = [Buf(f"hb{i}") for i in range(2)]
        gpsB = Buf("gp0")
        upsB = Buf("up0")

        def f1_load_xf(q):
            for g4 in range(4):
                P.dma("sp", xf_s[:].rearrange("p (c t) -> p c t", c=NCC)[:, g4::4, :],
                      xn_d.rearrange("(c p) t -> p c t", p=128)[:, g4::4, q * TQ:(q + 1) * TQ],
                      xfB, fresh=(g4 == 0), waits=dB["xn"].dep_read())

        def f1_load_guw(oc):
            op_ = oc % 2
            for g4 in range(4):
                P.dma("sp",
                      guw_s[op_][:].rearrange("p (c d) -> p c d", c=NCC)[:, g4::4, 0:128],
                      wgu_sd.rearrange("(c p) o -> p c o", p=128)[:, g4::4, oc * 128:(oc + 1) * 128],
                      guwB[op_], fresh=(g4 == 0), waits=dB["wgu"].dep_read())
                P.dma("sp",
                      guw_s[op_][:].rearrange("p (c d) -> p c d", c=NCC)[:, g4::4, 128:256],
                      wgu_sd.rearrange("(c p) o -> p c o", p=128)[:, g4::4, I + oc * 128:I + (oc + 1) * 128],
                      guwB[op_], fresh=False, waits=dB["wgu"].dep_read())

        def mk_f1_unit(q, oc):
            op_ = oc % 2

            def work():
                def gumm(e, gate):
                    last = None
                    col = 0 if gate else 128
                    tgt = gps if gate else ups
                    for cc in range(NCC):
                        for m in range(NMM):
                            last = e.matmul(
                                tgt[:, m * MW:(m + 1) * MW],
                                guw_s[op_][:, cc * 256 + col: cc * 256 + col + 128],
                                xf_s[:, cc * TQ + m * MW: cc * TQ + (m + 1) * MW],
                                start=(cc == 0), stop=(cc == NCC - 1))
                    return last
                P.op("pe", lambda e: gumm(e, True), rd=[guwB[op_], xfB], wr=[gpsB])
                P.op("pe", lambda e: gumm(e, False), rd=[guwB[op_], xfB], wr=[upsB])
                P.op("act", lambda e: e.activation(sil_s[:], gps[:], AF.Silu),
                     rd=[gpsB], wr=[silB])
                jp = oc % 2
                P.op("dve", lambda e, jp=jp: e.tensor_tensor(
                    hb_s[jp][:], sil_s[:], ups[:], ALU.mult),
                    rd=[silB, upsB], wr=[hbB[jp]])
                P.store("sp", h_sd[oc * 128:(oc + 1) * 128, q * TQ:(q + 1) * TQ],
                        hb_s[jp][:], hbB[jp], dB["h"])
            return (q, oc, work)

        f1_units = []
        if upto >= 4:
            for q in range(4):
                for oc in range(NOC):
                    f1_units.append(mk_f1_unit(q, oc))
        f1_state = {"i": 0}

        def f1_pump(k):
            for _ in range(k):
                i = f1_state["i"]
                if i >= len(f1_units):
                    return
                q, oc = f1_units[i][0], f1_units[i][1]
                if oc == 0:
                    f1_load_xf(q)
                    f1_load_guw(0)
                if oc + 1 < NOC:
                    f1_load_guw(oc + 1)
                f1_units[i][2]()
                f1_state["i"] = i + 1

        P.barrier(include_pool=False)

        a2 = contextlib.ExitStack()
        with a2:
            KCH = T // 128
            keb_s = mksb(a2, "keb", [DL, H * DH], BF16)
            veb_s = mksb(a2, "veb", [DL, H * DH], BF16)
            qh_s = [mksb(a2, f"qh{i}", [128, T], BF16) for i in range(2)]
            ktb_s = mksb(a2, "ktb", [128, T], BF16)
            vtb_s = mksb(a2, "vtb", [128, T], BF16)
            pt_s = [mksb(a2, f"pt{i}", [128, QT], BF16) for i in range(4)]
            pd_s = mksb(a2, "pd", [128, 128], BF16)
            lsum_s = mksb(a2, "lsum", [1, QT], F32)
            rsum_s = mksb(a2, "rsum", [1, QT], F32)
            rbcs_s = mksb(a2, "rbcs", [128, QT], F32)
            ytb_s = mksb(a2, "ytb", [128, T], BF16)

            sps = [mkps(a2, f"sp{i}", [128, QT]) for i in range(2)]
            yps = mkps(a2, "yps", [128, QT])
            smp = mkps(a2, "smp", [1, QT])

            kebB, vebB = Buf("keb"), Buf("veb")
            qhB = [Buf(f"qh{i}") for i in range(2)]
            ktbB, vtbB = Buf("ktb"), Buf("vtb")
            ptB = [Buf(f"pt{i}") for i in range(4)]
            pdB = Buf("pd")
            lsumB, rsumB, rbcsB = Buf("lsum"), Buf("rsum"), Buf("rbcs")
            ytbB = Buf("ytb")
            spsB = [Buf(f"sp{i}") for i in range(2)]
            ypsB, smpB = Buf("yps"), Buf("smp")

            with nc.sbuf_tensor("kvstage", [DL, H * DH], F32) as kvstage:
                kvsB = Buf("kvstage")
                P.dma("sp", kvstage[:].rearrange("p (h d) -> p h d", h=H),
                      ke_d.rearrange("(h p) d -> p h d", p=DL), kvsB)
                P.op("dve", lambda e: e.tensor_copy(keb_s[:], kvstage[:]),
                     rd=[kvsB], wr=[kebB])
                P.dma("sp", kvstage[:].rearrange("p (h d) -> p h d", h=H),
                      ve_d.rearrange("(h p) d -> p h d", p=DL), kvsB)
                P.op("dve", lambda e: e.tensor_copy(veb_s[:], kvstage[:]),
                     rd=[kvsB], wr=[vebB])

            for h in range(H if upto >= 2 else 0):
                hp = h % 2
                P.dma("sp", qh_s[hp][:], q_d[h * 128:(h + 1) * 128, :], qhB[hp],
                      waits=dB["q"].dep_read())
                for kb in range(T // 512):
                    kx = kb % 2
                    P.op("pe", lambda e, kb=kb, h=h, kx=kx: e.matmul(
                        sps[kx][:], keb_s[:, h * DH:(h + 1) * DH],
                        klat_s[:, kb * 512:(kb + 1) * 512], start=True, stop=True),
                        rd=[kebB, klatB], wr=[spsB[kx]])
                    P.op("act", lambda e, kb=kb, kx=kx: e.copy(
                        ktb_s[:, kb * 512:(kb + 1) * 512], sps[kx][:]),
                        rd=[spsB[kx]], wr=[ktbB], fresh=(kb == 0))
                for kc in range(KCH):
                    kx = kc % 2
                    P.op("pe", lambda e, kc=kc, h=h, kx=kx: e.matmul(
                        sps[kx][:, 0:128], vlat_s[:, kc * 128:(kc + 1) * 128],
                        veb_s[:, h * DH:(h + 1) * DH], start=True, stop=True),
                        rd=[vebB, vlatB], wr=[spsB[kx]])
                    P.op("dve", lambda e, kc=kc, kx=kx: e.tensor_copy(
                        vtb_s[:, kc * 128:(kc + 1) * 128], sps[kx][:, 0:128]),
                        rd=[spsB[kx]], wr=[vtbB], fresh=(kc == 0))
                if f0_state["i"] >= n_gate_units[0]:
                    f1_pump(1)
                for qt in range(NQT):
                    kmax = (qt * QT + QT) // 128

                    def scores(kc, qt=qt, hp=hp):
                        a = kc * 128 - qt * QT
                        q0 = max(0, a)
                        sp_ = kc % 2
                        pp = kc % 4
                        P.op("pe", lambda e, sp_=sp_, kc=kc, qt=qt, q0=q0, hp=hp: e.matmul(
                            sps[sp_][:, q0:QT], ktb_s[:, kc * 128:(kc + 1) * 128],
                            qh_s[hp][:, qt * QT + q0: qt * QT + QT],
                            start=True, stop=True), rd=[ktbB, qhB[hp]], wr=[spsB[sp_]])
                        if a >= 0:
                            P.op("act", lambda e, sp_=sp_, q0=q0: e.activation(
                                pd_s[:], sps[sp_][:, q0:q0 + 128], AF.Exp,
                                bias=cs["nbias"][:, 0:1], scale=SCALE),
                                rd=[spsB[sp_], cB["nbias"]], wr=[pdB])
                            P.op("dve", lambda e, pp=pp, q0=q0: e.tensor_tensor(
                                pt_s[pp][:, q0:q0 + 128], pd_s[:], cs["tri"][:], ALU.mult),
                                rd=[pdB, cB["tri"]], wr=[ptB[pp]])
                            if QT - q0 > 128:
                                P.op("act", lambda e, sp_=sp_, pp=pp, q0=q0: e.activation(
                                    pt_s[pp][:, q0 + 128:QT], sps[sp_][:, q0 + 128:QT],
                                    AF.Exp, bias=cs["nbias"][:, 0:1], scale=SCALE),
                                    rd=[spsB[sp_], cB["nbias"]], wr=[ptB[pp]], fresh=False)
                        else:
                            P.op("act", lambda e, sp_=sp_, pp=pp: e.activation(
                                pt_s[pp][:], sps[sp_][:], AF.Exp,
                                bias=cs["nbias"][:, 0:1], scale=SCALE),
                                rd=[spsB[sp_], cB["nbias"]], wr=[ptB[pp]])

                    def sums_pv_k(kc, qt=qt, kmax=kmax):
                        a = kc * 128 - qt * QT
                        q0 = max(0, a)
                        pp = kc % 4

                        def sums_pv(e, pp=pp, kc=kc, q0=q0, kmax=kmax):
                            e.matmul(smp[:, q0:QT], cs["onesb"][:], pt_s[pp][:, q0:QT],
                                     start=(kc == 0), stop=(kc == kmax - 1))
                            return e.matmul(yps[:, q0:QT],
                                            vtb_s[:, kc * 128:(kc + 1) * 128],
                                            pt_s[pp][:, q0:QT],
                                            start=(kc == 0), stop=(kc == kmax - 1))
                        P.op("pe", sums_pv, rd=[ptB[pp], cB["onesb"], vtbB],
                             wr=[smpB, ypsB], fresh=(kc == 0))

                    scores(0)
                    for kc in range(kmax):
                        if kc + 1 < kmax:
                            scores(kc + 1)
                        sums_pv_k(kc)
                    f0_pump(3)
                    if f0_state["i"] >= n_gate_units[0]:
                        f1_pump(1)
                    P.op("act", lambda e: e.activation(lsum_s[:], smp[:], AF.Ln),
                         rd=[smpB], wr=[lsumB])
                    P.op("act", lambda e: e.activation(rsum_s[:], lsum_s[:], AF.Exp,
                                                       scale=-1.0),
                         rd=[lsumB], wr=[rsumB])
                    P.op("pe", lambda e: e.matmul(sps[0][:], cs["onesf"][:], rsum_s[:],
                                                  start=True, stop=True),
                         rd=[rsumB, cB["onesf"]], wr=[spsB[0]])
                    P.op("dve", lambda e: e.tensor_copy(rbcs_s[:], sps[0][:]),
                         rd=[spsB[0]], wr=[rbcsB])
                    P.op("dve", lambda e, qt=qt: e.tensor_tensor(
                        ytb_s[:, qt * QT:(qt + 1) * QT], yps[:], rbcs_s[:], ALU.mult),
                        rd=[ypsB, rbcsB], wr=[ytbB], fresh=(qt == 0))
                P.store("act", y_d[h * 128:(h + 1) * 128, :], ytb_s[:], ytbB, dB["y"])

        f0_pump(len(f0_units))
        P.barrier(include_pool=True)

        a3 = contextlib.ExitStack()
        with a3:
            pwf_s = mksb(a3, "pwf0", [128, C], F32)
            pwb_s = mksb(a3, "pwb", [128, NCC * C], BF16)
            ysl_s = mksb(a3, "ysl0", [128, NCC * 512], BF16)
            asb_s = mksb(a3, "asb0", [128, C], F32)
            aps = [mkps(a3, f"ap{i}", [128, 1024]) for i in range(2)]

            pwfB = Buf("pwf0")
            pwbB = Buf("pwb")
            yslB = Buf("ysl0")
            asbB = Buf("asb0")
            apsB = [Buf(f"ap{i}") for i in range(2)]

            for hd in range(NCC):
                P.dma("sp", pwf_s[:], pw_d[hd * 128:(hd + 1) * 128, :], pwfB)
                P.op("dve", lambda e, hd=hd: e.tensor_copy(
                    pwb_s[:, hd * C:(hd + 1) * C], pwf_s[:]),
                    rd=[pwfB], wr=[pwbB], fresh=(hd == 0))
            for qg in range(T // 512 if upto >= 3 else 0):
                for g4 in range(4):
                    P.dma("sp",
                          ysl_s[:].rearrange("p (c t) -> p c t", c=NCC)[:, g4::4, :],
                          y_d.rearrange("(c p) t -> p c t", p=128)[:, g4::4, qg * 512:(qg + 1) * 512],
                          yslB, fresh=(g4 == 0), waits=dB["y"].dep_read())
                for jb in range(4):
                    qb = qg * 4 + jb

                    def pmm(e, jb=jb):
                        last = None
                        for ccs in range(4):
                            for hd in range(NCC):
                                last = e.matmul(
                                    aps[ccs // 2][:, (ccs % 2) * 512:(ccs % 2) * 512 + 512],
                                    ysl_s[:, hd * 512 + jb * 128: hd * 512 + jb * 128 + 128],
                                    pwb_s[:, hd * C + ccs * 512: hd * C + ccs * 512 + 512],
                                    start=(hd == 0), stop=(hd == NCC - 1))
                        return last
                    P.op("pe", pmm, rd=[yslB, pwbB], wr=[apsB[0], apsB[1]])
                    f1_pump(3)
                    P.op("act", lambda e: e.copy(asb_s[:, 0:1024], aps[0][:]),
                         rd=[apsB[0]], wr=[asbB])
                    P.op("act", lambda e: e.copy(asb_s[:, 1024:2048], aps[1][:]),
                         rd=[apsB[1]], wr=[asbB], fresh=False)
                    P.store("sp", attn_d[qb * 128:(qb + 1) * 128, :], asb_s[:],
                            asbB, dB["attn"])

        f1_pump(len(f1_units))
        f1s.close()
        f0s.close()
        acm.close()
        P.barrier(include_pool=True)

        c1 = contextlib.ExitStack()
        with c1:
            NDM = TT // 512 if TT >= 512 else 1
            DW = min(512, TT)
            NTB = TT // 128
            ht_s = [mksb(c1, f"ht{i}", [128, NOC * TT], BF16) for i in range(2)]
            dnw_s = [mksb(c1, f"dnw{i}", [128, NOC * 256], BF16) for i in range(2)]
            ybf_s = [mksb(c1, f"ybf{i}", [128, NCC * TT], BF16) for i in range(2)]
            y2_s = [mksb(c1, f"y2{i}", [128, TT], BF16) for i in range(2)]
            lr_s = mksb(c1, "lr", [1, TT], F32)
            rr_s = mksb(c1, "rr", [1, TT], F32)
            rcol_s = [mksb(c1, f"rcol{i}", [128, NTB], F32) for i in range(2)]
            xa_s = [mksb(c1, f"xa{i}", [128, C], F32) for i in range(2)]
            ob_s = [mksb(c1, f"ob{i}", [128, C], F32) for i in range(2)]
            yds = [mkps(c1, f"yd{i}", [128, TT]) for i in range(2)]
            smq2 = [mkps(c1, f"smq{i}", [1, TT]) for i in range(2)]
            tps2 = [mkps(c1, f"tq{i}", [128, 128], BF16) for i in range(2)]
            rcp = mkps(c1, "rcp", [128, 1])

            htB = [Buf(f"ht{i}") for i in range(2)]
            dnwB = [Buf(f"dnw{i}") for i in range(2)]
            ybfB = [Buf(f"ybf{i}") for i in range(2)]
            y2B = [Buf(f"y2{i}") for i in range(2)]
            lrB, rrB = Buf("lr"), Buf("rr")
            rcolB = [Buf(f"rcol{i}") for i in range(2)]
            xaB = [Buf(f"xa{i}") for i in range(2)]
            xcB = [Buf(f"xc{i}") for i in range(2)]
            obB = [Buf(f"ob{i}") for i in range(2)]
            ydsB = [Buf(f"yd{i}") for i in range(2)]
            smqB2 = [Buf(f"smq{i}") for i in range(2)]
            tpsB = [Buf(f"tq{i}") for i in range(2)]
            rcpB = Buf("rcp")

            pend_rstd = []
            pend_asm = []
            pend_sq = []
            for tt in range(NTT if upto >= 5 else 0):
                ttp = tt % 2
                tc0 = tt * TT
                for g4 in range(4):
                    P.dma("sp", ht_s[ttp][:].rearrange("p (c t) -> p c t", c=NOC)[:, g4::4, :],
                          h_sd.rearrange("(c p) t -> p c t", p=128)[:, g4::4, tc0:tc0 + TT],
                          htB[ttp], fresh=(g4 == 0), waits=dB["h"].dep_read())

                def rstd_cl(tt=tt, ttp=ttp):
                    P.op("act", lambda e, ttp=ttp: e.activation(
                        lr_s[:], smq2[ttp][:], AF.Ln,
                        bias=cs["epsr"][0:1, 0:1], scale=1.0 / C),
                        rd=[smqB2[ttp], cB["epsr"]], wr=[lrB])
                    P.op("act", lambda e: e.activation(rr_s[:], lr_s[:], AF.Exp, scale=-0.5),
                         rd=[lrB], wr=[rrB])
                    for tb in range(NTB):
                        P.op("pe", lambda e, tb=tb: e.matmul(
                            rcp[:], rr_s[:, tb * 128:(tb + 1) * 128], cs["onef1"][:],
                            start=True, stop=True), rd=[rrB, cB["onef1"]], wr=[rcpB])
                        P.op("dve", lambda e, tb=tb, ttp=ttp: e.tensor_copy(
                            rcol_s[ttp][:, tb:tb + 1], rcp[:]), rd=[rcpB],
                            wr=[rcolB[ttp]], fresh=(tb == 0))

                def asm_cl(tb, tt=tt, ttp=ttp):
                    def run():
                        bp = tb % 2
                        rows = tt * TT + tb * 128
                        evx = P.dma("sp", xa_s[bp][:], xin[rows:rows + 128, :], xaB[bp])
                        P.dma("pool", xa_s[bp][:], attn_d[rows:rows + 128, :], xcB[bp],
                              waits=dB["attn"].dep_read() + [evx], accum=True)
                        for cc in range(NCC):
                            tp = cc % 2
                            P.op("pe", lambda e, tp=tp, cc=cc, tb=tb, ttp=ttp: e.transpose(
                                tps2[tp][:],
                                ybf_s[ttp][:, cc * TT + tb * 128: cc * TT + tb * 128 + 128],
                                cs["ident"][:]), rd=[ybfB[ttp], cB["ident"]], wr=[tpsB[tp]])
                            P.op("dve", lambda e, tp=tp, cc=cc, bp=bp, tb=tb, ttp=ttp:
                                 e.scalar_tensor_tensor(
                                     ob_s[bp][:, cc * 128:(cc + 1) * 128], tps2[tp][:],
                                     rcol_s[ttp][:, tb:tb + 1],
                                     xa_s[bp][:, cc * 128:(cc + 1) * 128],
                                     ALU.mult, ALU.add),
                                 rd=[tpsB[tp], rcolB[ttp], xaB[bp], xcB[bp]],
                                 wr=[obB[bp]], fresh=(cc == 0))
                        P.store("sp", out_d[rows:rows + 128, :], ob_s[bp][:], obB[bp], outB)
                    return run

                def dnw_load(pair, tt=tt):
                    pp_ = pair % 2
                    for g4 in range(4):
                        P.dma("act",
                              dnw_s[pp_][:].rearrange("p (c d) -> p c d", c=NOC)[:, g4::4, :],
                              wdn_sd.rearrange("(c p) o -> p c o", p=128)[:, g4::4, pair * 256:(pair + 1) * 256],
                              dnwB[pp_], fresh=(g4 == 0), waits=dB["wdn"].dep_read())

                for cc in range(NCC):
                    ccp = (cc // 2) % 2
                    ydp = cc % 2
                    if cc % 2 == 0:
                        if tt == 0 and cc == 0:
                            dnw_load(0)
                            dnw_load(1)
                        elif cc // 2 + 1 < NCC // 2:
                            dnw_load(cc // 2 + 1)
                        elif tt + 1 < NTT:
                            dnw_load(0)


                    def dmm(e, ccp=ccp, ydp=ydp, cc=cc, ttp=ttp):
                        last = None
                        half = (cc % 2) * 128
                        for ic in range(NOC):
                            for m in range(NDM):
                                last = e.matmul(
                                    yds[ydp][:, m * DW:(m + 1) * DW],
                                    dnw_s[ccp][:, ic * 256 + half: ic * 256 + half + 128],
                                    ht_s[ttp][:, ic * TT + m * DW:(ic) * TT + (m + 1) * DW],
                                    start=(ic == 0), stop=(ic == NOC - 1))
                        return last
                    P.op("pe", dmm, rd=[dnwB[ccp], htB[ttp]], wr=[ydsB[ydp]])
                    if cc == 1 and pend_rstd:
                        pend_rstd.pop()()
                    if cc >= 4 and cc % 3 == 1 and pend_asm:
                        pend_asm.pop(0)()
                    P.op("dve", lambda e, cc=cc, ydp=ydp, ttp=ttp: e.tensor_copy(
                        ybf_s[ttp][:, cc * TT:(cc + 1) * TT], yds[ydp][:]),
                        rd=[ydsB[ydp]], wr=[ybfB[ttp]], fresh=(cc == 0))
                    P.op("act", lambda e, cc=cc, ttp=ttp: e.activation(
                        y2_s[cc % 2][:], ybf_s[ttp][:, cc * TT:(cc + 1) * TT], AF.Square),
                        rd=[ybfB[ttp]], wr=[y2B[cc % 2]])

                    def sq_cl(cc=cc, ttp=ttp, y2e=None):
                        def sqmm(e, cc=cc, ttp=ttp):
                            last = None
                            for m in range(NDM):
                                last = e.matmul(smq2[ttp][:, m * DW:(m + 1) * DW],
                                                cs["onesb"][:],
                                                y2_s[cc % 2][:, m * DW:(m + 1) * DW],
                                                start=(cc == 0), stop=(cc == NCC - 1))
                            return last
                        P.op("pe", sqmm, rd=[y2B[cc % 2], cB["onesb"]], wr=[smqB2[ttp]],
                             fresh=(cc == 0))
                    pend_sq.append(sq_cl)
                    if len(pend_sq) > 1:
                        pend_sq.pop(0)()
                while pend_sq:
                    pend_sq.pop(0)()
                pend_rstd.append(rstd_cl)
                for tb in range(NTB):
                    pend_asm.append(asm_cl(tb))
            while pend_rstd:
                pend_rstd.pop()()
            while pend_asm:
                pend_asm.pop(0)()

        P.final_wait(outB)

        with nc.Block() as block:
            P.emit(block)
    return nc


def kernel(**inputs):
    cfg = Cfg()
    prep = host_prep(inputs, cfg)
    x = np.ascontiguousarray(np.asarray(inputs["L0_attn_x"], np.float32))
    nc = build_program(cfg)
    in_maps = []
    for b in range(8):
        m = {"x": np.ascontiguousarray(x[b]),
             "wq": prep["wq"], "wk": prep["wk"], "wv": prep["wv"],
             "ke": prep["ke"], "ve": prep["ve"], "pw": prep["pw"],
             "gub": prep["gub"], "gus": prep["gus"],
             "dnb": prep["dnb"], "dns": prep["dns"]}
        in_maps.append(m)
    res = run_bass_kernel_spmd(nc, in_maps, core_ids=list(range(8)))
    return np.stack([np.asarray(res.results[b]["out"]) for b in range(8)], axis=0)


# version 14
# speedup vs baseline: 1.2419x; 1.0786x over previous
import contextlib
import numpy as np
import ml_dtypes

import concourse.bass as bass
import concourse.mybir as mybir
from concourse.bass_utils import run_bass_kernel_spmd

F32 = mybir.dt.float32
BF16 = mybir.dt.bfloat16
F16 = mybir.dt.float16
U16 = mybir.dt.uint16
I32 = mybir.dt.int32
AF = mybir.ActivationFunctionType
ALU = mybir.AluOpType
AX = mybir.AxisListType
NPBF16 = ml_dtypes.bfloat16


class Cfg:
    def __init__(self, T=4096, I=5632):
        self.C = 2048
        self.T = T
        self.H = 16
        self.DH = 128
        self.DL = 64
        self.I = I
        self.NCC = self.C // 128
        self.SLAB = T // 8 if T >= 1024 else 128
        self.NSLAB = T // self.SLAB
        self.QT = 512
        self.NQT = T // 512
        self.TT = T // 8
        self.NTT = 8
        self.NOC = I // 128
        self.NBC_GU = self.C // 256
        self.NBC_DN = I // 256
        self.EXP_SHIFT = 12.0
        self.EPS_LN = 1e-5
        self.EPS_RMS = 1e-6
        assert T % 512 == 0 and self.SLAB % 128 == 0 and I % 256 == 0


def _perm_half(n):
    p = np.empty(n, np.int64)
    for f in range(n):
        p[(f & 1) * (n // 2) + (f >> 1)] = f
    return p


def host_prep(inputs, cfg):
    C, H, DH, DL, I = cfg.C, cfg.H, cfg.DH, cfg.DL, cfg.I
    fo_gu = _perm_half(C)
    fo_dn = _perm_half(I)
    do_h = _perm_half(DH)
    lo_l = _perm_half(DL)

    d = {}
    wq = np.asarray(inputs["L0_attn_wq"])
    col_perm = np.concatenate([h * DH + do_h for h in range(H)])
    d["wq"] = np.ascontiguousarray(wq[np.ix_(fo_gu, col_perm)], np.float32)
    d["wk"] = np.ascontiguousarray(
        np.asarray(inputs["L0_attn_wk_lat"])[np.ix_(fo_gu, lo_l)], np.float32)
    d["wv"] = np.ascontiguousarray(np.asarray(inputs["L0_attn_wv_lat"])[fo_gu], np.float32)
    ke = np.asarray(inputs["L0_attn_k_expand"])[:, lo_l][:, :, do_h]
    d["ke"] = np.ascontiguousarray(ke.reshape(H * DL, DH), np.float32)
    d["ve"] = np.ascontiguousarray(
        np.asarray(inputs["L0_attn_v_expand"]).reshape(H * DL, DH), np.float32)
    d["pw"] = np.ascontiguousarray(np.asarray(inputs["L0_attn_proj_w"]), np.float32)

    gup = np.asarray(inputs["L0_ffn_gate_up_packed"])
    gus = np.asarray(inputs["L0_ffn_gate_up_scales"])
    dnp_ = np.asarray(inputs["L0_ffn_down_packed"])
    dns = np.asarray(inputs["L0_ffn_down_scales"])

    def widen(b):
        w = np.zeros(b.shape + (2,), np.uint8)
        w[..., 0] = b
        return np.ascontiguousarray(w.view(np.uint16)[..., 0])

    gu_col = np.concatenate([fo_dn, I + fo_dn])
    gu_bytes = gup.view(np.uint8).reshape(2 * I, C // 2)
    d["gub"] = widen(np.ascontiguousarray(gu_bytes[gu_col].T))
    d["gus"] = np.ascontiguousarray(gus[:, gu_col])
    dn_bytes = dnp_.view(np.uint8).reshape(C, I // 2)
    d["dnb"] = widen(np.ascontiguousarray(dn_bytes.T))
    d["dns"] = np.ascontiguousarray(dns)
    return d


def rope_tables(t, dim):
    inv_freq = 1.0 / 10000.0 ** (np.arange(0, dim, 2, dtype=np.float32) / dim)
    ang = np.arange(t, dtype=np.float32)[:, None] * inv_freq[None, :]
    return (np.ascontiguousarray(np.cos(ang).T).astype(NPBF16),
            np.ascontiguousarray(np.sin(ang).T).astype(NPBF16))


def rot_mat_T(dim):
    half = dim // 2
    m = np.zeros((dim, dim), NPBF16)
    for i in range(half):
        m[i, half + i] = 1.0
        m[half + i, i] = -1.0
    return m


ENGS = ("pe", "act", "dve", "pool", "sp")


class Buf:
    __slots__ = ("name", "w", "r", "war", "dsem", "dcnt", "pool_dma")

    def __init__(self, name):
        self.name = name
        self.w = []
        self.r = []
        self.war = []
        self.dsem = None
        self.dcnt = 0
        self.pool_dma = False

    def dep_write(self, fresh):
        if fresh:
            self.war = self.w + self.r
            self.w = []
            self.r = []
        return list(self.war)

    def dep_read(self):
        return list(self.w)


class Prog:
    def __init__(self, nc, stack):
        self.nc = nc
        self.stack = stack
        self.sem = {e: stack.enter_context(nc.semaphore(f"S_{e}")) for e in ENGS}
        self.cnt = {e: 0 for e in ENGS}
        self.recs = {e: [] for e in ENGS}
        self.all_dma_bufs = []

    def op(self, eng, fn, rd=(), wr=(), fresh=True, waits=()):
        wts = list(waits)
        for b in rd:
            wts += b.dep_read()
        for b in wr:
            wts += b.dep_write(fresh)
        self.cnt[eng] += 1
        ev = (self.sem[eng], self.cnt[eng])
        self.recs[eng].append((wts, fn, ev, 1))
        for b in rd:
            b.r.append(ev)
        for b in wr:
            b.w.append(ev)
        return ev

    def dma(self, queue, out_ap, in_ap, dst, src=(), fresh=True, waits=(), dram_dst=None,
            accum=False):
        if dst.dsem is None:
            dst.dsem = self.stack.enter_context(self.nc.semaphore(f"D_{dst.name}"))
            self.all_dma_bufs.append(dst)
            if queue == "pool":
                dst.pool_dma = True
        wts = list(waits)
        for b in src:
            wts += b.dep_read()
        wts += dst.dep_write(fresh)
        if dram_dst is not None:
            wts += dram_dst.dep_write(False)
        dst.dcnt += 16
        ev = (dst.dsem, dst.dcnt)

        def fn(e, _o=out_ap, _i=in_ap, _a=accum):
            if _a:
                return e.dma_start(out=_o, in_=_i, accum_op=mybir.AluOpType.add)
            return e.dma_start(out=_o, in_=_i)

        self.recs[queue].append((wts, fn, ev, 16))
        for b in src:
            b.r.append(ev)
        dst.w.append(ev)
        if dram_dst is not None:
            dram_dst.w.append(ev)
        return ev

    def store(self, queue, out_ap, in_ap, src_buf, dram_dst):
        key = (id(src_buf), id(dram_dst))
        if not hasattr(self, "_stslots"):
            self._stslots = {}
        slot = self._stslots.get(key)
        if slot is None:
            slot = Buf("stsl_" + src_buf.name + "_" + dram_dst.name)
            self._stslots[key] = slot
        return self.dma(queue, out_ap, in_ap, slot, src=[src_buf], fresh=False,
                        dram_dst=dram_dst)

    def barrier(self, include_pool=True):
        engs = [e for e in ENGS if include_pool or e != "pool"]
        wts = [(self.sem[e], self.cnt[e]) for e in engs if self.cnt[e] > 0]
        wts += [(b.dsem, b.dcnt) for b in self.all_dma_bufs
                if include_pool or not b.pool_dma]
        self.cnt["sp"] += 1
        ev = (self.sem["sp"], self.cnt["sp"])
        self.recs["sp"].append((wts, lambda e: e.nop(), ev, 1))
        for e in engs:
            if e == "sp":
                continue
            self.cnt[e] += 1
            self.recs[e].append(([ev], lambda en: en.nop(), (self.sem[e], self.cnt[e]), 1))

    def final_wait(self, buf):
        self.recs["sp"].append((buf.dep_read(), None, None, 0))

    def emit(self, block):
        hmap = {"pe": block.tensor, "act": block.scalar, "dve": block.vector,
                "pool": block.gpsimd, "sp": block.sync}

        def mk(eng):
            recs = self.recs[eng]

            def f(e):
                seen = {}
                for wts, fn, ev, inc in recs:
                    byname = {}
                    for (sem, val) in wts:
                        k = id(sem)
                        if byname.get(k, (None, -1))[1] < val:
                            byname[k] = (sem, val)
                    for k, (sem, val) in byname.items():
                        if seen.get(k, -1) >= val:
                            continue
                        e.wait_ge(sem, val)
                        seen[k] = val
                    if fn is None:
                        continue
                    inst = fn(e)
                    if inc:
                        inst.then_inc(ev[0], inc)
            return f

        for eng in ENGS:
            hmap[eng](mk(eng))


def build_program(cfg, dbg=False, upto=9, f2n=9):
    nc = bass.Bass("TRN2", target_bir_lowering=False)
    C, T, H, DH, DL, I = cfg.C, cfg.T, cfg.H, cfg.DH, cfg.DL, cfg.I
    NCC, SLAB, NSLAB = cfg.NCC, cfg.SLAB, cfg.NSLAB
    QT, NQT, TT, NTT, NOC = cfg.QT, cfg.NQT, cfg.TT, cfg.NTT, cfg.NOC
    SCALE = float(1.0 / np.sqrt(DH))

    xin = nc.dram_tensor("x", [T, C], F32, kind="ExternalInput").ap()
    wq_d = nc.dram_tensor("wq", [C, C], F32, kind="ExternalInput").ap()
    wk_d = nc.dram_tensor("wk", [C, DL], F32, kind="ExternalInput").ap()
    wv_d = nc.dram_tensor("wv", [C, DL], F32, kind="ExternalInput").ap()
    ke_d = nc.dram_tensor("ke", [H * DL, DH], F32, kind="ExternalInput").ap()
    ve_d = nc.dram_tensor("ve", [H * DL, DH], F32, kind="ExternalInput").ap()
    pw_d = nc.dram_tensor("pw", [C, C], F32, kind="ExternalInput").ap()
    gub_d = nc.dram_tensor("gub", [C // 2, 2 * I], U16, kind="ExternalInput").ap()
    gus_d = nc.dram_tensor("gus", [C // 128, 2 * I], F16, kind="ExternalInput").ap()
    dnb_d = nc.dram_tensor("dnb", [I // 2, C], U16, kind="ExternalInput").ap()
    dns_d = nc.dram_tensor("dns", [I // 128, C], F16, kind="ExternalInput").ap()
    out_d = nc.dram_tensor("out", [T, C], F32, kind="ExternalOutput").ap()

    sk = "ExternalOutput" if dbg else "Internal"
    xn_d = nc.dram_tensor("xn_s", [C, T], BF16, kind=sk).ap()
    q_d = nc.dram_tensor("q_s", [C, T], BF16, kind=sk).ap()
    y_d = nc.dram_tensor("y_s", [C, T], BF16, kind=sk).ap()
    attn_d = nc.dram_tensor("attn_s", [T, C], F32, kind=sk).ap()
    wgu_sd = nc.dram_tensor("wgu_s", [C, 2 * I], BF16, kind=sk).ap()
    wdn_sd = nc.dram_tensor("wdn_s", [I, C], BF16, kind=sk).ap()
    h_sd = nc.dram_tensor("h_s", [I, T], BF16, kind=sk).ap()
    wqb_sd = nc.dram_tensor("wqb_s", [128, (C // 128) * C], BF16, kind="Internal").ap()

    cqn, sqn = rope_tables(T, DH)
    ckn, skn = rope_tables(T, DL)
    consts_np = {
        "ident": np.eye(128, dtype=NPBF16),
        "tri": np.triu(np.ones((128, 128), NPBF16)),
        "onesb": np.ones((128, 1), NPBF16),
        "onesf": np.ones((1, 128), np.float32),
        "onef1": np.ones((1, 1), np.float32),
        "cqb": np.vstack([cqn, cqn]), "sqb": np.vstack([sqn, sqn]),
        "ckb": np.vstack([ckn, ckn]), "skb": np.vstack([skn, skn]),
        "rotq": rot_mat_T(DH), "rotk": rot_mat_T(DL),
    }
    consts_d = {k: nc.inline_tensor(v, name=k + "_c") for k, v in consts_np.items()}

    stack = contextlib.ExitStack()
    with stack:
        P = Prog(nc, stack)

        def mksb(st_, name, shape, dt):
            return st_.enter_context(nc.sbuf_tensor(name, shape, dt))

        def mkps(st_, name, shape, dt=F32):
            return st_.enter_context(nc.psum_tensor(name, shape, dt))

        cs = {}
        cB = {}
        ACM_CONSTS = ("cqb", "sqb", "ckb", "skb", "tri", "rotq", "rotk")
        for k, v in consts_np.items():
            if k in ACM_CONSTS:
                continue
            cs[k] = mksb(stack, k, list(v.shape), BF16 if v.dtype == NPBF16 else F32)
            cB[k] = Buf(k)
            P.dma("sp", cs[k][:], consts_d[k].ap(), cB[k])
        for k, val in (("nbias", -cfg.EXP_SHIFT), ("epsb", cfg.EPS_LN), ("epsr", cfg.EPS_RMS)):
            cs[k] = mksb(stack, k, [128, 1], F32)
            cB[k] = Buf(k)
            P.op("dve", lambda e, k=k, val=val: e.memset(cs[k][:], val), wr=[cB[k]])

        dB = {k: Buf(k) for k in ("xn", "q", "y", "attn", "wgu", "wdn", "h", "wqb")}
        outB = Buf("out")

        acm = contextlib.ExitStack()
        stack.enter_context(acm)
        for k in ("tri",):
            v = consts_np[k]
            cs[k] = mksb(acm, k, list(v.shape), BF16 if v.dtype == NPBF16 else F32)
            cB[k] = Buf(k)
            P.dma("sp", cs[k][:], consts_d[k].ap(), cB[k])
        klat_s = mksb(acm, "klat", [DL, T], BF16)
        vlat_s = mksb(acm, "vlat", [DL, T], BF16)
        klatB, vlatB = Buf("klat"), Buf("vlat")

        OSL = 1024
        f0s = contextlib.ExitStack()
        stack.enter_context(f0s)
        f0t = {}
        for nm, dt in (("d1", U16), ("d2a", U16), ("d2b", U16), ("d3a", BF16),
                       ("d3b", BF16), ("d4", U16), ("d5a", BF16), ("d5b", BF16),
                       ("p1a", BF16), ("p1b", BF16), ("p2", BF16)):
            f0t[nm] = (mksb(f0s, "f0" + nm, [128, OSL], dt), Buf("f0" + nm))
        f0byt = [(mksb(f0s, f"f0byt{i}", [128, OSL], U16), Buf(f"f0byt{i}")) for i in range(2)]
        f0sm = [(mksb(f0s, f"f0sm{i}", [128, OSL], F16), Buf(f"f0sm{i}")) for i in range(2)]
        f0wo = [(mksb(f0s, f"f0wo{i}", [128, OSL], BF16), Buf(f"f0wo{i}")) for i in range(2)]

        f0_units = []

        def mk_f0_unit(bytes_d, scales_d, w_out_d, wB, ncols, bc, sl, hi, ui):
            c0 = sl * OSL
            n = min(ncols, c0 + OSL) - c0
            byt, bB = f0byt[(ui // 2) % 2]
            smt, sB = f0sm[(ui // 2) % 2]
            sh = 2 if hi else 6
            msk = 0x80 if hi else 0x8
            sgs = -0.015625 if hi else -0.25
            ab = "b" if (ui % 2) else "a"
            d1, b1 = f0t["d1"]
            d2, b2 = f0t["d2" + ab]
            d3, b3 = f0t["d3" + ab]
            d4, b4 = f0t["d4"]
            d5, b5 = f0t["d5" + ab]
            p1, pb1 = f0t["p1" + ab]
            p2, pb2 = f0t["p2"]
            wo, woB = f0wo[ui % 2]

            def load():
                if not hi:
                    P.dma("pool", byt[:, 0:n], bytes_d[bc * 128:(bc + 1) * 128, c0:c0 + n], bB)
                    P.dma("pool", smt[0:64, 0:n],
                          scales_d[2 * bc:2 * bc + 1, c0:c0 + n].broadcast_to((64, n)), sB)
                    P.dma("pool", smt[64:128, 0:n],
                          scales_d[2 * bc + 1:2 * bc + 2, c0:c0 + n].broadcast_to((64, n)),
                          sB, fresh=False)

            def work():
                orow = bc * 128 + hi * ((C if w_out_d is wgu_sd else I) // 2)
                nh = n // 2
                mask32 = 0x01C001C0
                msk32 = (msk << 16) | msk
                sgsh = 12 if msk == 0x8 else 8
                P.op("dve", lambda e, nh=nh: e.tensor_scalar(
                    d1[:, 0:n].bitcast(I32), byt[:, 0:n].bitcast(I32), sh, mask32,
                    ALU.logical_shift_left, ALU.bitwise_and), rd=[bB], wr=[b1])
                P.op("dve", lambda e: e.tensor_scalar(
                    d2[:, 0:n], d1[:, 0:n], 0x3F00, None, ALU.add), rd=[b1], wr=[b2])
                P.op("dve", lambda e: e.tensor_scalar(
                    d3[:, 0:n], d2[:, 0:n].bitcast(BF16), 2.0, 1.0,
                    ALU.mult, ALU.subtract), rd=[b2], wr=[b3])
                P.op("dve", lambda e, msk32=msk32: e.tensor_scalar(
                    d4[:, 0:n].bitcast(I32), byt[:, 0:n].bitcast(I32), msk32, None,
                    ALU.bitwise_and), rd=[bB], wr=[b4])
                P.op("dve", lambda e, sgsh=sgsh: e.tensor_scalar(
                    d5[:, 0:n].bitcast(I32), d4[:, 0:n].bitcast(I32), sgsh, 0x3F803F80,
                    ALU.logical_shift_left, ALU.bitwise_or), rd=[b4], wr=[b5])
                P.op("dve", lambda e: e.tensor_tensor(
                    p1[:, 0:n], d2[:, 0:n].bitcast(BF16), d3[:, 0:n], ALU.min),
                    rd=[b2, b3], wr=[pb1])
                P.op("pool", lambda e: e.tensor_tensor(
                    p2[:, 0:n], p1[:, 0:n], d5[:, 0:n], ALU.mult),
                    rd=[pb1, b5], wr=[pb2])
                P.op("pool", lambda e: e.tensor_tensor(
                    wo[:, 0:n], p2[:, 0:n], smt[:, 0:n], ALU.mult),
                    rd=[pb2, sB], wr=[woB])
                P.store("pool", w_out_d[orow:orow + 128, c0:c0 + n], wo[:, 0:n], woB, wB)
            return load, work

        ui = 0
        n_gate_units = [0]
        for (bytes_d, scales_d, w_out_d, wB, ncols, nbc) in (
                (gub_d, gus_d, wgu_sd, dB["wgu"], 2 * I, cfg.NBC_GU),
                (dnb_d, dns_d, wdn_sd, dB["wdn"], C, cfg.NBC_DN)):
            nsl = (ncols + OSL - 1) // OSL
            for bc in range(nbc):
                for sl in range(nsl):
                    for hi in (0, 1):
                        f0_units.append(mk_f0_unit(bytes_d, scales_d, w_out_d, wB,
                                                   ncols, bc, sl, hi, ui))
                        ui += 1
            if w_out_d is wgu_sd:
                n_gate_units[0] = ui

        f0_state = {"i": 0}

        def f0_pump(k):
            for _ in range(k):
                i = f0_state["i"]
                if i >= len(f0_units):
                    return
                if i == 0:
                    f0_units[0][0]()
                    f0_units[1][0]()
                if i + 2 < len(f0_units):
                    f0_units[i + 2][0]()
                f0_units[i][1]()
                f0_state["i"] = i + 1


        a1 = contextlib.ExitStack()
        with a1:
            for k in ("cqb", "sqb", "ckb", "skb", "rotq", "rotk"):
                v = consts_np[k]
                cs[k] = mksb(a1, k, list(v.shape), BF16 if v.dtype == NPBF16 else F32)
                cB[k] = Buf(k)
                P.dma("sp", cs[k][:], consts_d[k].ap(), cB[k])
            XT = SLAB // 128
            NQM = max(1, SLAB // 512)
            QW = min(512, SLAB)
            xt_s = [mksb(a1, f"xts{i}", [128, C], F32) for i in range(2)]
            sq_s2 = [mksb(a1, f"sqs{i}", [128, C], BF16) for i in range(2)]
            xnm_s = [mksb(a1, f"xnm{i}", [128, C], BF16) for i in range(2)]
            st_s2 = [mksb(a1, f"sts{i}", [128, 10], F32) for i in range(2)]
            xnT_s = [mksb(a1, f"xnT{i}", [128, NCC * SLAB], BF16) for i in range(2)]
            wqf_s = [mksb(a1, f"wqf{i}", [128, NCC * 128], F32) for i in range(2)]
            wqb_s = [mksb(a1, f"wqb{i}", [128, NCC * 128], BF16) for i in range(2)]
            qsc_s2 = [mksb(a1, f"qsc{i}", [128, SLAB], BF16) for i in range(2)]
            qro_s = [mksb(a1, f"qro{i}", [128, SLAB], BF16) for i in range(2)]
            rs_s = [mksb(a1, f"rs{i}", [128, SLAB], BF16) for i in range(2)]
            wkvf_s = mksb(a1, "wkvf", [128, NCC * 2 * DL], F32)
            wkvb_s = mksb(a1, "wkvb", [128, NCC * 2 * DL], BF16)
            kvsc_s = mksb(a1, "kvsc", [DL, SLAB], BF16)

            trps = [mkps(a1, f"trp{i}", [128, 128], BF16) for i in range(4)]
            qps = [mkps(a1, f"qp{i}", [128, SLAB]) for i in range(2)]
            kvps = mkps(a1, "kvp", [DL, SLAB])
            rops = mkps(a1, "rop", [128, SLAB])

            xtB = [Buf(f"xts{i}") for i in range(2)]
            sqB2 = [Buf(f"sqs{i}") for i in range(2)]
            xnmB = [Buf(f"xnm{i}") for i in range(2)]
            stB2 = [[Buf(f"st{j}_{i}") for i in range(10)] for j in range(2)]
            xnTB = [Buf(f"xnT{i}") for i in range(2)]
            wqfB = [Buf(f"wqf{i}") for i in range(2)]
            wqbB = [Buf(f"wqb{i}") for i in range(2)]
            qscB2 = [Buf(f"qsc{i}") for i in range(2)]
            qroB = [Buf(f"qro{i}") for i in range(2)]
            rsB = [Buf(f"rs{i}") for i in range(2)]
            wkvfB = Buf("wkvf")
            wkvbB = Buf("wkvb")
            kvscB = Buf("kvsc")
            trpB = [Buf(f"trp{i}") for i in range(4)]
            qpsB = [Buf(f"qp{i}") for i in range(2)]
            kvpsB = Buf("kvp")
            ropsB = Buf("rop")

            P.dma("sp", wkvf_s[:, 0:NCC * DL].rearrange("p (c d) -> p c d", c=NCC),
                  wk_d.rearrange("(c p) d -> p c d", p=128), wkvfB)
            P.dma("sp", wkvf_s[:, NCC * DL:2 * NCC * DL].rearrange("p (c d) -> p c d", c=NCC),
                  wv_d.rearrange("(c p) d -> p c d", p=128), wkvfB, fresh=False)
            P.op("dve", lambda e: e.tensor_copy(wkvb_s[:], wkvf_s[:]),
                 rd=[wkvfB], wr=[wkvbB])

            for s in range(NSLAB if upto >= 1 else 0):
                p = s % 2
                t0 = s * SLAB
                for xt in range(XT):
                    xp = xt % 2
                    rows = t0 + xt * 128
                    P.dma("sp", xt_s[xp][:], xin[rows:rows + 128, :], xtB[xp])
                    st_s = st_s2[xp]
                    stB = stB2[xp]
                    sq_l = sq_s2[xp]
                    sqB = sqB2[xp]
                    P.op("dve", lambda e, xp=xp, st_s=st_s: e.tensor_reduce(
                        st_s[:, 0:1], xt_s[xp][:], AX.X, ALU.add, negate=True),
                        rd=[xtB[xp]], wr=[stB[0]])
                    P.op("act", lambda e, xp=xp, st_s=st_s, sq_l=sq_l: e.activation(
                        sq_l[:], xt_s[xp][:], AF.Square, accum_out=st_s[:, 1:2]),
                        rd=[xtB[xp]], wr=[sqB, stB[1]])
                    P.op("dve", lambda e, st_s=st_s: e.tensor_scalar(
                        st_s[:, 2:3], st_s[:, 0:1], 1.0 / C, None, ALU.mult),
                        rd=[stB[0]], wr=[stB[2]])
                    P.op("dve", lambda e, st_s=st_s: e.tensor_scalar(
                        st_s[:, 3:4], st_s[:, 1:2], 1.0 / C, None, ALU.mult),
                        rd=[stB[1]], wr=[stB[3]])
                    P.op("dve", lambda e, st_s=st_s: e.tensor_scalar(
                        st_s[:, 4:5], st_s[:, 2:3], st_s[:, 2:3], None, ALU.mult),
                        rd=[stB[2]], wr=[stB[4]])
                    P.op("dve", lambda e, st_s=st_s: e.tensor_tensor(
                        st_s[:, 5:6], st_s[:, 3:4], st_s[:, 4:5], ALU.subtract),
                        rd=[stB[3], stB[4]], wr=[stB[5]])
                    P.op("act", lambda e, st_s=st_s: e.activation(
                        st_s[:, 6:7], st_s[:, 5:6], AF.Sqrt, bias=cs["epsb"][:, 0:1]),
                        rd=[stB[5], cB["epsb"]], wr=[stB[6]])
                    P.op("dve", lambda e, st_s=st_s: e.reciprocal(st_s[:, 7:8], st_s[:, 6:7]),
                         rd=[stB[6]], wr=[stB[7]])
                    P.op("dve", lambda e, xp=xp, st_s=st_s: e.tensor_scalar(
                        xnm_s[xp][:], xt_s[xp][:], st_s[:, 2:3], st_s[:, 7:8],
                        ALU.add, ALU.mult), rd=[xtB[xp], stB[2], stB[7]], wr=[xnmB[xp]])
                    f0_pump(2)
                    for cc in range(NCC):
                        tp = cc % 4
                        par = cc // (NCC // 2)
                        base = (cc % (NCC // 2)) * 256 + par
                        P.op("pe", lambda e, tp=tp, xp=xp, base=base: e.transpose(
                            trps[tp][:], xnm_s[xp][:, base:base + 255:2], cs["ident"][:]),
                            rd=[xnmB[xp], cB["ident"]], wr=[trpB[tp]])
                        if cc % 4 >= 2:
                            P.op("dve", lambda e, tp=tp, p=p, cc=cc, xt=xt: e.tensor_copy(
                                xnT_s[p][:, cc * SLAB + xt * 128: cc * SLAB + xt * 128 + 128],
                                trps[tp][:]), rd=[trpB[tp]], wr=[xnTB[p]],
                                fresh=(cc == 0 and xt == 0))
                        else:
                            P.op("act", lambda e, tp=tp, p=p, cc=cc, xt=xt: e.copy(
                                xnT_s[p][:, cc * SLAB + xt * 128: cc * SLAB + xt * 128 + 128],
                                trps[tp][:]), rd=[trpB[tp]], wr=[xnTB[p]],
                                fresh=(cc == 0 and xt == 0))
                for g4 in range(4):
                    P.store("act",
                            xn_d.rearrange("(c p) t -> p c t", p=128)[:, g4::4, t0:t0 + SLAB],
                            xnT_s[p][:].rearrange("p (c t) -> p c t", c=NCC)[:, g4::4, :],
                            xnTB[p], dB["xn"])
                def rope_tail(dc, t0):
                    qp_ = dc % 2
                    qsc_l, qscB_l = qsc_s2[qp_], qscB2[qp_]
                    P.op("pe", lambda e, qsc_l=qsc_l: e.matmul(
                        rops[:], cs["rotq"][:], qsc_l[:], start=True, stop=True),
                        rd=[qscB_l, cB["rotq"]], wr=[ropsB])
                    P.op("dve", lambda e, t0=t0, qsc_l=qsc_l: e.tensor_tensor(
                        rs_s[0][:], qsc_l[:], cs["cqb"][:, t0:t0 + SLAB], ALU.mult),
                        rd=[qscB_l, cB["cqb"]], wr=[rsB[0]])
                    P.op("dve", lambda e, t0=t0: e.tensor_tensor(
                        rs_s[1][:], rops[:], cs["sqb"][:, t0:t0 + SLAB], ALU.mult),
                        rd=[ropsB, cB["sqb"]], wr=[rsB[1]])
                    P.op("dve", lambda e, qp_=qp_: e.tensor_tensor(
                        qro_s[qp_][:], rs_s[0][:], rs_s[1][:], ALU.add),
                        rd=[rsB[0], rsB[1]], wr=[qroB[qp_]])
                    P.store("act", q_d[dc * 128:(dc + 1) * 128, t0:t0 + SLAB],
                            qro_s[qp_][:], qroB[qp_], dB["q"])

                for dc in range(H):
                    dp = dc % 2
                    if s == 0:
                        P.dma("sp", wqf_s[dp][:].rearrange("p (c d) -> p c d", c=NCC),
                              wq_d.rearrange("(c p) d -> p c d", p=128)[:, :, dc * 128:(dc + 1) * 128],
                              wqfB[dp])
                        P.op("dve", lambda e, dp=dp: e.tensor_copy(wqb_s[dp][:], wqf_s[dp][:]),
                             rd=[wqfB[dp]], wr=[wqbB[dp]])
                        P.store("act", wqb_sd[:, dc * C:(dc + 1) * C],
                                wqb_s[dp][:], wqbB[dp], dB["wqb"])
                    else:
                        P.dma("sp", wqb_s[dp][:], wqb_sd[:, dc * C:(dc + 1) * C],
                              wqbB[dp], waits=dB["wqb"].dep_read())

                    def qmm(e, dp=dp, p=p):
                        last = None
                        for cc in range(NCC):
                            for j in range(NQM):
                                last = e.matmul(
                                    qps[dp][:, j * QW:(j + 1) * QW],
                                    wqb_s[dp][:, cc * 128:(cc + 1) * 128],
                                    xnT_s[p][:, cc * SLAB + j * QW: cc * SLAB + (j + 1) * QW],
                                    start=(cc == 0), stop=(cc == NCC - 1))
                        return last
                    P.op("pe", qmm, rd=[wqbB[dp], xnTB[p]], wr=[qpsB[dp]])
                    P.op("act", lambda e, dp=dp: e.copy(qsc_s2[dp][:], qps[dp][:]),
                         rd=[qpsB[dp]], wr=[qscB2[dp]])
                    f0_pump(1)
                    if dc >= 1:
                        rope_tail(dc - 1, t0)
                rope_tail(H - 1, t0)
                for which in (0, 1):
                    def kvmm(e, which=which, p=p):
                        last = None
                        for cc in range(NCC):
                            for j in range(NQM):
                                last = e.matmul(
                                    kvps[:, j * QW:(j + 1) * QW],
                                    wkvb_s[:, which * NCC * DL + cc * DL:
                                           which * NCC * DL + (cc + 1) * DL],
                                    xnT_s[p][:, cc * SLAB + j * QW: cc * SLAB + (j + 1) * QW],
                                    start=(cc == 0), stop=(cc == NCC - 1))
                        return last
                    P.op("pe", kvmm, rd=[wkvbB, xnTB[p]], wr=[kvpsB])
                    if which == 1:
                        P.op("act", lambda e, t0=t0: e.copy(
                            vlat_s[:, t0:t0 + SLAB], kvps[:]),
                            rd=[kvpsB], wr=[vlatB], fresh=(s == 0))
                    else:
                        P.op("act", lambda e: e.copy(kvsc_s[:], kvps[:]),
                             rd=[kvpsB], wr=[kvscB])
                        P.op("pe", lambda e: e.matmul(rops[0:DL, :], cs["rotk"][:],
                                                      kvsc_s[:], start=True, stop=True),
                             rd=[kvscB, cB["rotk"]], wr=[ropsB])
                        P.op("dve", lambda e, t0=t0: e.tensor_tensor(
                            rs_s[0][0:DL, :], kvsc_s[:], cs["ckb"][:, t0:t0 + SLAB],
                            ALU.mult), rd=[kvscB, cB["ckb"]], wr=[rsB[0]])
                        P.op("dve", lambda e, t0=t0: e.tensor_tensor(
                            rs_s[1][0:DL, :], rops[0:DL, :], cs["skb"][:, t0:t0 + SLAB],
                            ALU.mult), rd=[ropsB, cB["skb"]], wr=[rsB[1]])
                        P.op("dve", lambda e, t0=t0: e.tensor_tensor(
                            klat_s[:, t0:t0 + SLAB], rs_s[0][0:DL, :], rs_s[1][0:DL, :],
                            ALU.add), rd=[rsB[0], rsB[1]], wr=[klatB], fresh=(s == 0))

        f1s = contextlib.ExitStack()
        stack.enter_context(f1s)
        TQ = T // 4
        NMM = TQ // 512 if TQ >= 512 else 1
        MW = min(512, TQ)
        xf_s = mksb(f1s, "xfq", [128, NCC * TQ], BF16)
        guw_s = [mksb(f1s, f"guw{i}", [128, NCC * 256], BF16) for i in range(2)]
        sil_s = mksb(f1s, "sil", [128, TQ], BF16)
        hb_s = [mksb(f1s, f"hb{i}", [128, TQ], BF16) for i in range(2)]
        gps = mkps(f1s, "gp0", [128, TQ])
        ups = mkps(f1s, "up0", [128, TQ])
        xfB = Buf("xfq")
        guwB = [Buf(f"guw{i}") for i in range(2)]
        silB = Buf("sil")
        hbB = [Buf(f"hb{i}") for i in range(2)]
        gpsB = Buf("gp0")
        upsB = Buf("up0")

        def f1_load_xf(q):
            for g4 in range(4):
                P.dma("sp", xf_s[:].rearrange("p (c t) -> p c t", c=NCC)[:, g4::4, :],
                      xn_d.rearrange("(c p) t -> p c t", p=128)[:, g4::4, q * TQ:(q + 1) * TQ],
                      xfB, fresh=(g4 == 0), waits=dB["xn"].dep_read())

        def f1_load_guw(oc):
            op_ = oc % 2
            for g4 in range(4):
                P.dma("sp",
                      guw_s[op_][:].rearrange("p (c d) -> p c d", c=NCC)[:, g4::4, 0:128],
                      wgu_sd.rearrange("(c p) o -> p c o", p=128)[:, g4::4, oc * 128:(oc + 1) * 128],
                      guwB[op_], fresh=(g4 == 0), waits=dB["wgu"].dep_read())
                P.dma("sp",
                      guw_s[op_][:].rearrange("p (c d) -> p c d", c=NCC)[:, g4::4, 128:256],
                      wgu_sd.rearrange("(c p) o -> p c o", p=128)[:, g4::4, I + oc * 128:I + (oc + 1) * 128],
                      guwB[op_], fresh=False, waits=dB["wgu"].dep_read())

        def mk_f1_unit(q, oc):
            op_ = oc % 2

            def work():
                def gumm(e, gate):
                    last = None
                    col = 0 if gate else 128
                    tgt = gps if gate else ups
                    for cc in range(NCC):
                        for m in range(NMM):
                            last = e.matmul(
                                tgt[:, m * MW:(m + 1) * MW],
                                guw_s[op_][:, cc * 256 + col: cc * 256 + col + 128],
                                xf_s[:, cc * TQ + m * MW: cc * TQ + (m + 1) * MW],
                                start=(cc == 0), stop=(cc == NCC - 1))
                    return last
                P.op("pe", lambda e: gumm(e, True), rd=[guwB[op_], xfB], wr=[gpsB])
                P.op("pe", lambda e: gumm(e, False), rd=[guwB[op_], xfB], wr=[upsB])
                P.op("act", lambda e: e.activation(sil_s[:], gps[:], AF.Silu),
                     rd=[gpsB], wr=[silB])
                jp = oc % 2
                P.op("dve", lambda e, jp=jp: e.tensor_tensor(
                    hb_s[jp][:], sil_s[:], ups[:], ALU.mult),
                    rd=[silB, upsB], wr=[hbB[jp]])
                P.store("sp", h_sd[oc * 128:(oc + 1) * 128, q * TQ:(q + 1) * TQ],
                        hb_s[jp][:], hbB[jp], dB["h"])
            return (q, oc, work)

        f1_units = []
        if upto >= 4:
            for q in range(4):
                for oc in range(NOC):
                    f1_units.append(mk_f1_unit(q, oc))
        f1_state = {"i": 0}

        def f1_pump(k):
            for _ in range(k):
                i = f1_state["i"]
                if i >= len(f1_units):
                    return
                q, oc = f1_units[i][0], f1_units[i][1]
                if oc == 0:
                    f1_load_xf(q)
                    f1_load_guw(0)
                if oc + 1 < NOC:
                    f1_load_guw(oc + 1)
                f1_units[i][2]()
                f1_state["i"] = i + 1

        P.barrier(include_pool=False)

        a2 = contextlib.ExitStack()
        with a2:
            KCH = T // 128
            keb_s = mksb(a2, "keb", [DL, H * DH], BF16)
            veb_s = mksb(a2, "veb", [DL, H * DH], BF16)
            qh_s = [mksb(a2, f"qh{i}", [128, T], BF16) for i in range(2)]
            ktb_s = mksb(a2, "ktb", [128, T], BF16)
            vtb_s = mksb(a2, "vtb", [128, T], BF16)
            pt_s = [mksb(a2, f"pt{i}", [128, QT], BF16) for i in range(4)]
            pd_s = mksb(a2, "pd", [128, 128], BF16)
            lsum_s = mksb(a2, "lsum", [1, QT], F32)
            rsum_s = mksb(a2, "rsum", [1, QT], F32)
            rbcs_s = mksb(a2, "rbcs", [128, QT], F32)
            ytb_s = mksb(a2, "ytb", [128, T], BF16)

            sps = [mkps(a2, f"sp{i}", [128, QT]) for i in range(2)]
            yps = mkps(a2, "yps", [128, QT])
            smp = mkps(a2, "smp", [1, QT])

            kebB, vebB = Buf("keb"), Buf("veb")
            qhB = [Buf(f"qh{i}") for i in range(2)]
            ktbB, vtbB = Buf("ktb"), Buf("vtb")
            ptB = [Buf(f"pt{i}") for i in range(4)]
            pdB = Buf("pd")
            lsumB, rsumB, rbcsB = Buf("lsum"), Buf("rsum"), Buf("rbcs")
            ytbB = Buf("ytb")
            spsB = [Buf(f"sp{i}") for i in range(2)]
            ypsB, smpB = Buf("yps"), Buf("smp")

            with nc.sbuf_tensor("kvstage", [DL, H * DH], F32) as kvstage:
                kvsB = Buf("kvstage")
                P.dma("sp", kvstage[:].rearrange("p (h d) -> p h d", h=H),
                      ke_d.rearrange("(h p) d -> p h d", p=DL), kvsB)
                P.op("dve", lambda e: e.tensor_copy(keb_s[:], kvstage[:]),
                     rd=[kvsB], wr=[kebB])
                P.dma("sp", kvstage[:].rearrange("p (h d) -> p h d", h=H),
                      ve_d.rearrange("(h p) d -> p h d", p=DL), kvsB)
                P.op("dve", lambda e: e.tensor_copy(veb_s[:], kvstage[:]),
                     rd=[kvsB], wr=[vebB])

            for h in range(H if upto >= 2 else 0):
                hp = h % 2
                P.dma("sp", qh_s[hp][:], q_d[h * 128:(h + 1) * 128, :], qhB[hp],
                      waits=dB["q"].dep_read())
                for kb in range(T // 512):
                    kx = kb % 2
                    P.op("pe", lambda e, kb=kb, h=h, kx=kx: e.matmul(
                        sps[kx][:], keb_s[:, h * DH:(h + 1) * DH],
                        klat_s[:, kb * 512:(kb + 1) * 512], start=True, stop=True),
                        rd=[kebB, klatB], wr=[spsB[kx]])
                    P.op("act", lambda e, kb=kb, kx=kx: e.copy(
                        ktb_s[:, kb * 512:(kb + 1) * 512], sps[kx][:]),
                        rd=[spsB[kx]], wr=[ktbB], fresh=(kb == 0))
                for kc in range(KCH):
                    kx = kc % 2
                    P.op("pe", lambda e, kc=kc, h=h, kx=kx: e.matmul(
                        sps[kx][:, 0:128], vlat_s[:, kc * 128:(kc + 1) * 128],
                        veb_s[:, h * DH:(h + 1) * DH], start=True, stop=True),
                        rd=[vebB, vlatB], wr=[spsB[kx]])
                    P.op("dve", lambda e, kc=kc, kx=kx: e.tensor_copy(
                        vtb_s[:, kc * 128:(kc + 1) * 128], sps[kx][:, 0:128]),
                        rd=[spsB[kx]], wr=[vtbB], fresh=(kc == 0))
                if f0_state["i"] >= n_gate_units[0]:
                    f1_pump(1)
                for qt in range(NQT):
                    kmax = (qt * QT + QT) // 128

                    def scores(kc, qt=qt, hp=hp):
                        a = kc * 128 - qt * QT
                        q0 = max(0, a)
                        sp_ = kc % 2
                        pp = kc % 4
                        P.op("pe", lambda e, sp_=sp_, kc=kc, qt=qt, q0=q0, hp=hp: e.matmul(
                            sps[sp_][:, q0:QT], ktb_s[:, kc * 128:(kc + 1) * 128],
                            qh_s[hp][:, qt * QT + q0: qt * QT + QT],
                            start=True, stop=True), rd=[ktbB, qhB[hp]], wr=[spsB[sp_]])
                        if a >= 0:
                            P.op("act", lambda e, sp_=sp_, q0=q0: e.activation(
                                pd_s[:], sps[sp_][:, q0:q0 + 128], AF.Exp,
                                bias=cs["nbias"][:, 0:1], scale=SCALE),
                                rd=[spsB[sp_], cB["nbias"]], wr=[pdB])
                            P.op("dve", lambda e, pp=pp, q0=q0: e.tensor_tensor(
                                pt_s[pp][:, q0:q0 + 128], pd_s[:], cs["tri"][:], ALU.mult),
                                rd=[pdB, cB["tri"]], wr=[ptB[pp]])
                            if QT - q0 > 128:
                                P.op("act", lambda e, sp_=sp_, pp=pp, q0=q0: e.activation(
                                    pt_s[pp][:, q0 + 128:QT], sps[sp_][:, q0 + 128:QT],
                                    AF.Exp, bias=cs["nbias"][:, 0:1], scale=SCALE),
                                    rd=[spsB[sp_], cB["nbias"]], wr=[ptB[pp]], fresh=False)
                        else:
                            P.op("act", lambda e, sp_=sp_, pp=pp: e.activation(
                                pt_s[pp][:], sps[sp_][:], AF.Exp,
                                bias=cs["nbias"][:, 0:1], scale=SCALE),
                                rd=[spsB[sp_], cB["nbias"]], wr=[ptB[pp]])

                    def sums_pv_k(kc, qt=qt, kmax=kmax):
                        a = kc * 128 - qt * QT
                        q0 = max(0, a)
                        pp = kc % 4

                        def sums_pv(e, pp=pp, kc=kc, q0=q0, kmax=kmax):
                            e.matmul(smp[:, q0:QT], cs["onesb"][:], pt_s[pp][:, q0:QT],
                                     start=(kc == 0), stop=(kc == kmax - 1))
                            return e.matmul(yps[:, q0:QT],
                                            vtb_s[:, kc * 128:(kc + 1) * 128],
                                            pt_s[pp][:, q0:QT],
                                            start=(kc == 0), stop=(kc == kmax - 1))
                        P.op("pe", sums_pv, rd=[ptB[pp], cB["onesb"], vtbB],
                             wr=[smpB, ypsB], fresh=(kc == 0))

                    scores(0)
                    for kc in range(kmax):
                        if kc + 1 < kmax:
                            scores(kc + 1)
                        sums_pv_k(kc)
                    f0_pump(3)
                    if f0_state["i"] >= n_gate_units[0]:
                        f1_pump(1)
                    P.op("act", lambda e: e.activation(lsum_s[:], smp[:], AF.Ln),
                         rd=[smpB], wr=[lsumB])
                    P.op("act", lambda e: e.activation(rsum_s[:], lsum_s[:], AF.Exp,
                                                       scale=-1.0),
                         rd=[lsumB], wr=[rsumB])
                    P.op("pe", lambda e: e.matmul(sps[0][:], cs["onesf"][:], rsum_s[:],
                                                  start=True, stop=True),
                         rd=[rsumB, cB["onesf"]], wr=[spsB[0]])
                    P.op("dve", lambda e: e.tensor_copy(rbcs_s[:], sps[0][:]),
                         rd=[spsB[0]], wr=[rbcsB])
                    P.op("dve", lambda e, qt=qt: e.tensor_tensor(
                        ytb_s[:, qt * QT:(qt + 1) * QT], yps[:], rbcs_s[:], ALU.mult),
                        rd=[ypsB, rbcsB], wr=[ytbB], fresh=(qt == 0))
                P.store("act", y_d[h * 128:(h + 1) * 128, :], ytb_s[:], ytbB, dB["y"])

        f0_pump(len(f0_units))
        P.barrier(include_pool=True)

        a3 = contextlib.ExitStack()
        with a3:
            pwf_s = mksb(a3, "pwf0", [128, C], F32)
            pwb_s = mksb(a3, "pwb", [128, NCC * C], BF16)
            ysl_s = mksb(a3, "ysl0", [128, NCC * 512], BF16)
            asb_s = mksb(a3, "asb0", [128, C], F32)
            aps = [mkps(a3, f"ap{i}", [128, 1024]) for i in range(2)]

            pwfB = Buf("pwf0")
            pwbB = Buf("pwb")
            yslB = Buf("ysl0")
            asbB = Buf("asb0")
            apsB = [Buf(f"ap{i}") for i in range(2)]

            for hd in range(NCC):
                P.dma("sp", pwf_s[:], pw_d[hd * 128:(hd + 1) * 128, :], pwfB)
                P.op("dve", lambda e, hd=hd: e.tensor_copy(
                    pwb_s[:, hd * C:(hd + 1) * C], pwf_s[:]),
                    rd=[pwfB], wr=[pwbB], fresh=(hd == 0))
            for qg in range(T // 512 if upto >= 3 else 0):
                for g4 in range(4):
                    P.dma("sp",
                          ysl_s[:].rearrange("p (c t) -> p c t", c=NCC)[:, g4::4, :],
                          y_d.rearrange("(c p) t -> p c t", p=128)[:, g4::4, qg * 512:(qg + 1) * 512],
                          yslB, fresh=(g4 == 0), waits=dB["y"].dep_read())
                for jb in range(4):
                    qb = qg * 4 + jb

                    def pmm(e, jb=jb):
                        last = None
                        for ccs in range(4):
                            for hd in range(NCC):
                                last = e.matmul(
                                    aps[ccs // 2][:, (ccs % 2) * 512:(ccs % 2) * 512 + 512],
                                    ysl_s[:, hd * 512 + jb * 128: hd * 512 + jb * 128 + 128],
                                    pwb_s[:, hd * C + ccs * 512: hd * C + ccs * 512 + 512],
                                    start=(hd == 0), stop=(hd == NCC - 1))
                        return last
                    P.op("pe", pmm, rd=[yslB, pwbB], wr=[apsB[0], apsB[1]])
                    f1_pump(3)
                    P.op("act", lambda e: e.copy(asb_s[:, 0:1024], aps[0][:]),
                         rd=[apsB[0]], wr=[asbB])
                    P.op("act", lambda e: e.copy(asb_s[:, 1024:2048], aps[1][:]),
                         rd=[apsB[1]], wr=[asbB], fresh=False)
                    P.store("sp", attn_d[qb * 128:(qb + 1) * 128, :], asb_s[:],
                            asbB, dB["attn"])

        f1_pump(len(f1_units))
        f1s.close()
        f0s.close()
        acm.close()
        P.barrier(include_pool=True)

        c1 = contextlib.ExitStack()
        with c1:
            NDM = TT // 512 if TT >= 512 else 1
            DW = min(512, TT)
            NTB = TT // 128
            ht_s = [mksb(c1, f"ht{i}", [128, NOC * TT], BF16) for i in range(2)]
            dnw_s = [mksb(c1, f"dnw{i}", [128, NOC * 256], BF16) for i in range(2)]
            ybf_s = [mksb(c1, f"ybf{i}", [128, NCC * TT], BF16) for i in range(2)]
            y2_s = [mksb(c1, f"y2{i}", [128, TT], BF16) for i in range(2)]
            lr_s = mksb(c1, "lr", [1, TT], F32)
            rr_s = mksb(c1, "rr", [1, TT], F32)
            rcol_s = [mksb(c1, f"rcol{i}", [128, NTB], F32) for i in range(2)]
            xa_s = [mksb(c1, f"xa{i}", [128, C], F32) for i in range(2)]
            ob_s = [mksb(c1, f"ob{i}", [128, C], F32) for i in range(2)]
            yds = [mkps(c1, f"yd{i}", [128, TT]) for i in range(2)]
            smq2 = [mkps(c1, f"smq{i}", [1, TT]) for i in range(2)]
            tps2 = [mkps(c1, f"tq{i}", [128, 128], BF16) for i in range(2)]
            rcp = mkps(c1, "rcp", [128, 1])

            htB = [Buf(f"ht{i}") for i in range(2)]
            dnwB = [Buf(f"dnw{i}") for i in range(2)]
            ybfB = [Buf(f"ybf{i}") for i in range(2)]
            y2B = [Buf(f"y2{i}") for i in range(2)]
            lrB, rrB = Buf("lr"), Buf("rr")
            rcolB = [Buf(f"rcol{i}") for i in range(2)]
            xaB = [Buf(f"xa{i}") for i in range(2)]
            xcB = [Buf(f"xc{i}") for i in range(2)]
            obB = [Buf(f"ob{i}") for i in range(2)]
            ydsB = [Buf(f"yd{i}") for i in range(2)]
            smqB2 = [Buf(f"smq{i}") for i in range(2)]
            tpsB = [Buf(f"tq{i}") for i in range(2)]
            rcpB = Buf("rcp")

            pend_rstd = []
            pend_asm = []
            pend_sq = []
            for tt in range(NTT if upto >= 5 else 0):
                ttp = tt % 2
                tc0 = tt * TT
                for g4 in range(4):
                    P.dma("sp", ht_s[ttp][:].rearrange("p (c t) -> p c t", c=NOC)[:, g4::4, :],
                          h_sd.rearrange("(c p) t -> p c t", p=128)[:, g4::4, tc0:tc0 + TT],
                          htB[ttp], fresh=(g4 == 0), waits=dB["h"].dep_read())

                def rstd_cl(tt=tt, ttp=ttp):
                    P.op("act", lambda e, ttp=ttp: e.activation(
                        lr_s[:], smq2[ttp][:], AF.Ln,
                        bias=cs["epsr"][0:1, 0:1], scale=1.0 / C),
                        rd=[smqB2[ttp], cB["epsr"]], wr=[lrB])
                    P.op("act", lambda e: e.activation(rr_s[:], lr_s[:], AF.Exp, scale=-0.5),
                         rd=[lrB], wr=[rrB])
                    for tb in range(NTB):
                        P.op("pe", lambda e, tb=tb: e.matmul(
                            rcp[:], rr_s[:, tb * 128:(tb + 1) * 128], cs["onef1"][:],
                            start=True, stop=True), rd=[rrB, cB["onef1"]], wr=[rcpB])
                        P.op("dve", lambda e, tb=tb, ttp=ttp: e.tensor_copy(
                            rcol_s[ttp][:, tb:tb + 1], rcp[:]), rd=[rcpB],
                            wr=[rcolB[ttp]], fresh=(tb == 0))

                def asm_cl(tb, tt=tt, ttp=ttp):
                    def run():
                        bp = tb % 2
                        rows = tt * TT + tb * 128
                        evx = P.dma("sp", xa_s[bp][:], xin[rows:rows + 128, :], xaB[bp])
                        P.dma("pool", xa_s[bp][:], attn_d[rows:rows + 128, :], xcB[bp],
                              waits=dB["attn"].dep_read() + [evx], accum=True)
                        for cc in range(NCC):
                            tp = cc % 2
                            P.op("pe", lambda e, tp=tp, cc=cc, tb=tb, ttp=ttp: e.transpose(
                                tps2[tp][:],
                                ybf_s[ttp][:, cc * TT + tb * 128: cc * TT + tb * 128 + 128],
                                cs["ident"][:]), rd=[ybfB[ttp], cB["ident"]], wr=[tpsB[tp]])
                            P.op("dve", lambda e, tp=tp, cc=cc, bp=bp, tb=tb, ttp=ttp:
                                 e.scalar_tensor_tensor(
                                     ob_s[bp][:, cc * 128:(cc + 1) * 128], tps2[tp][:],
                                     rcol_s[ttp][:, tb:tb + 1],
                                     xa_s[bp][:, cc * 128:(cc + 1) * 128],
                                     ALU.mult, ALU.add),
                                 rd=[tpsB[tp], rcolB[ttp], xaB[bp], xcB[bp]],
                                 wr=[obB[bp]], fresh=(cc == 0))
                        P.store("sp", out_d[rows:rows + 128, :], ob_s[bp][:], obB[bp], outB)
                    return run

                def dnw_load(pair, tt=tt):
                    pp_ = pair % 2
                    for g4 in range(4):
                        P.dma("act",
                              dnw_s[pp_][:].rearrange("p (c d) -> p c d", c=NOC)[:, g4::4, :],
                              wdn_sd.rearrange("(c p) o -> p c o", p=128)[:, g4::4, pair * 256:(pair + 1) * 256],
                              dnwB[pp_], fresh=(g4 == 0), waits=dB["wdn"].dep_read())

                for cc in range(NCC):
                    ccp = (cc // 2) % 2
                    ydp = cc % 2
                    if cc % 2 == 0:
                        if tt == 0 and cc == 0:
                            dnw_load(0)
                            dnw_load(1)
                        elif cc // 2 + 1 < NCC // 2:
                            dnw_load(cc // 2 + 1)
                        elif tt + 1 < NTT:
                            dnw_load(0)


                    def dmm(e, ccp=ccp, ydp=ydp, cc=cc, ttp=ttp):
                        last = None
                        half = (cc % 2) * 128
                        for ic in range(NOC):
                            for m in range(NDM):
                                last = e.matmul(
                                    yds[ydp][:, m * DW:(m + 1) * DW],
                                    dnw_s[ccp][:, ic * 256 + half: ic * 256 + half + 128],
                                    ht_s[ttp][:, ic * TT + m * DW:(ic) * TT + (m + 1) * DW],
                                    start=(ic == 0), stop=(ic == NOC - 1))
                        return last
                    P.op("pe", dmm, rd=[dnwB[ccp], htB[ttp]], wr=[ydsB[ydp]])
                    if cc == 1 and pend_rstd:
                        pend_rstd.pop()()
                    if cc >= 4 and cc % 3 == 1 and pend_asm:
                        pend_asm.pop(0)()
                    P.op("dve", lambda e, cc=cc, ydp=ydp, ttp=ttp: e.tensor_copy(
                        ybf_s[ttp][:, cc * TT:(cc + 1) * TT], yds[ydp][:]),
                        rd=[ydsB[ydp]], wr=[ybfB[ttp]], fresh=(cc == 0))
                    P.op("act", lambda e, cc=cc, ttp=ttp: e.activation(
                        y2_s[cc % 2][:], ybf_s[ttp][:, cc * TT:(cc + 1) * TT], AF.Square),
                        rd=[ybfB[ttp]], wr=[y2B[cc % 2]])

                    def sq_cl(cc=cc, ttp=ttp, y2e=None):
                        def sqmm(e, cc=cc, ttp=ttp):
                            last = None
                            for m in range(NDM):
                                last = e.matmul(smq2[ttp][:, m * DW:(m + 1) * DW],
                                                cs["onesb"][:],
                                                y2_s[cc % 2][:, m * DW:(m + 1) * DW],
                                                start=(cc == 0), stop=(cc == NCC - 1))
                            return last
                        P.op("pe", sqmm, rd=[y2B[cc % 2], cB["onesb"]], wr=[smqB2[ttp]],
                             fresh=(cc == 0))
                    pend_sq.append(sq_cl)
                    if len(pend_sq) > 1:
                        pend_sq.pop(0)()
                while pend_sq:
                    pend_sq.pop(0)()
                pend_rstd.append(rstd_cl)
                for tb in range(NTB):
                    pend_asm.append(asm_cl(tb))
            while pend_rstd:
                pend_rstd.pop()()
            while pend_asm:
                pend_asm.pop(0)()

        P.final_wait(outB)

        with nc.Block() as block:
            P.emit(block)
    return nc


def kernel(**inputs):
    cfg = Cfg()
    prep = host_prep(inputs, cfg)
    x = np.ascontiguousarray(np.asarray(inputs["L0_attn_x"], np.float32))
    nc = build_program(cfg)
    in_maps = []
    for b in range(8):
        m = {"x": np.ascontiguousarray(x[b]),
             "wq": prep["wq"], "wk": prep["wk"], "wv": prep["wv"],
             "ke": prep["ke"], "ve": prep["ve"], "pw": prep["pw"],
             "gub": prep["gub"], "gus": prep["gus"],
             "dnb": prep["dnb"], "dns": prep["dns"]}
        in_maps.append(m)
    res = run_bass_kernel_spmd(nc, in_maps, core_ids=list(range(8)))
    return np.stack([np.asarray(res.results[b]["out"]) for b in range(8)], axis=0)
